# version 08
# speedup vs baseline: 1.2387x; 1.0242x over previous
import math
import numpy as np

import concourse.bass as bass
import concourse.mybir as mybir
from concourse.alu_op_type import AluOpType as OP
from concourse.bass_utils import run_bass_kernel_spmd


def _install_neff_cache():
    import hashlib
    import os
    import shutil
    try:
        from concourse import bass2jax
    except Exception:
        return
    if getattr(bass2jax, "_neff_cache_installed", False):
        return
    orig = bass2jax.compile_bir_kernel
    cache_dir = "/tmp/neff_build_cache"

    def cached(ant_bir_str, compile_dir_path, neff_name="file.neff", **kw):
        try:
            os.makedirs(cache_dir, exist_ok=True)
            h = hashlib.sha256(ant_bir_str if isinstance(ant_bir_str, bytes)
                               else ant_bir_str.encode()).hexdigest()[:24]
            cpath = os.path.join(cache_dir, h + ".neff")
            if os.path.exists(cpath):
                dst = os.path.join(compile_dir_path, neff_name)
                shutil.copyfile(cpath, dst)
                return dst
        except Exception:
            return orig(ant_bir_str, compile_dir_path, neff_name=neff_name, **kw)
        out = orig(ant_bir_str, compile_dir_path, neff_name=neff_name, **kw)
        try:
            tmp = cpath + ".tmp"
            shutil.copyfile(out, tmp)
            os.replace(tmp, cpath)
        except Exception:
            pass
        return out

    bass2jax.compile_bir_kernel = cached
    bass2jax._neff_cache_installed = True


_install_neff_cache()

F32 = mybir.dt.float32
BF16 = mybir.dt.bfloat16
AF = mybir.ActivationFunctionType
AXX = mybir.AxisListType.X

B, T, D = 4, 4096, 2048
TC = 2048
NSLOT = 8
SLOT = 512
NKC = 32
NDC = 16
NEG = -1.0e30

ENGINES = ("pe", "dve", "act", "pool", "syn")
BLOCK_ATTR = {"pe": "tensor", "dve": "vector", "act": "scalar", "pool": "gpsimd", "syn": "sync"}



NDSEM = 12


class Prog:

    def __init__(self):
        self.recs = {e: [] for e in ENGINES}
        self.cnt = {e: 0 for e in ENGINES}
        self.dma_rr = {}
        self.dma_cnt = {}
        self.resw = {}
        self.resr = {}
        self.floor = {}

    def phase_fence(self):
        for (e, c) in self.all_toks():
            if c > self.floor.get(e, 0):
                self.floor[e] = c
        self.resw = {}
        self.resr = {}

    def _record(self, eng, fn, waits, sem_name, inc):
        merged = {}
        for (e, c) in waits:
            if c and c > merged.get(e, 0):
                merged[e] = c
        self.recs[eng].append((fn, tuple(merged.items()), sem_name, inc))

    def step(self, eng, fn, reads=(), writes=(), extra=(), inc=1, dma=False):
        waits = [w for w in extra if not (eng == "pe" and w[0] == "pe")]
        waits.extend(self.floor.items())
        def keep(w):
            return not (eng == "pe" and w[0] == "pe")
        for k in reads:
            w = self.resw.get(k)
            if w and keep(w):
                waits.append(w)
        for k in writes:
            w = self.resw.get(k)
            if w and keep(w):
                waits.append(w)
            waits.extend(t for t in self.resr.get(k, ()) if keep(t))
        if dma:
            rr = self.dma_rr.get(eng, 0)
            self.dma_rr[eng] = (rr + 1) % NDSEM
            sem_name = f"{eng}_d{rr}"
            prev = self.dma_cnt.get(sem_name, 0)
            if prev:
                waits.append((sem_name, 16 * prev))
            self.dma_cnt[sem_name] = prev + 1
            tok = (sem_name, 16 * (prev + 1))
            self._record(eng, fn, waits, sem_name, 16)
        elif inc:
            self.cnt[eng] += 1
            tok = (eng, self.cnt[eng])
            self._record(eng, fn, waits, eng, 1)
        else:
            tok = (eng, self.cnt[eng] + 1)
            self._record(eng, fn, waits, None, 0)
        for k in reads:
            self.resr.setdefault(k, []).append(tok)
        for k in writes:
            self.resw[k] = tok
            self.resr[k] = []
        return tok

    def tok(self, eng):
        return (eng, self.cnt[eng])

    def all_toks(self):
        toks = [(e, self.cnt[e]) for e in ENGINES if self.cnt.get(e, 0) > 0]
        toks += [(sname, 16 * v) for sname, v in self.dma_cnt.items() if sname != "cc"]
        if self.cnt.get("ccn", 0) > 0:
            toks.append(("cc", self.cnt["ccn"]))
        return toks

    def sem_names(self):
        names = list(ENGINES)
        for e in ENGINES:
            for i in range(NDSEM):
                names.append(f"{e}_d{i}")
        names.append("cc")
        return names

    def barrier_key(self, *keys):
        for k in keys:
            self.resw[k] = None
        toks = self.all_toks()
        for k in keys:
            self.resr[k] = list(toks)
            self.resw[k] = None


def _replay_body(q, recs, sems, ename):
    maxw = {}
    for fn, waits, sem_name, inc in recs:
        for (pname, cntv) in waits:
            if cntv > maxw.get(pname, 0):
                q.wait_ge(sems[pname], cntv)
                maxw[pname] = cntv
        inst = fn(q)
        if inst is not None and inc and sem_name is not None:
            inst.then_inc(sems[sem_name], inc)


def replay(Pr, block, sems):
    for ename in ENGINES:
        recs = Pr.recs[ename]
        getattr(block, BLOCK_ATTR[ename])(
            lambda q, recs=recs, ename=ename: _replay_body(q, recs, sems, ename))


def DMA(P, eng, out, in_, reads=(), writes=(), extra=()):
    return P.step(eng, lambda q: q.dma_start(out=out, in_=in_), reads, writes, extra, dma=True)


def MM(P, out, lhsT, rhs, start, stop, reads=(), writes=(), extra=(), incall=False):
    return P.step("pe", lambda q: q.matmul(out, lhsT, rhs, start=start, stop=stop),
                  reads, writes, extra, inc=1 if (stop or incall) else 0)


def TT(P, out, in0, in1, op, reads=(), writes=(), extra=(), eng="dve"):
    return P.step(eng, lambda q: q.tensor_tensor(out=out, in0=in0, in1=in1, op=op),
                  reads, writes, extra, 1)


def TS(P, out, in0, s1, s2, op0, op1=None, reads=(), writes=(), extra=(), eng="dve"):
    def f(q):
        if op1 is None:
            return q.tensor_scalar(out=out, in0=in0, scalar1=s1, scalar2=None, op0=op0)
        return q.tensor_scalar(out=out, in0=in0, scalar1=s1, scalar2=s2, op0=op0, op1=op1)
    return P.step(eng, f, reads, writes, extra, 1)


def STT(P, out, in0, scalar, in1, op0, op1, reads=(), writes=(), extra=()):
    return P.step("dve", lambda q: q.scalar_tensor_tensor(out=out, in0=in0, scalar=scalar, in1=in1, op0=op0, op1=op1),
                  reads, writes, extra, 1)


def ACTV(P, out, in_, func, bias=None, scale=None, accum_out=None, reads=(), writes=(), extra=()):
    kw = {}
    if bias is not None:
        kw["bias"] = bias
    if scale is not None:
        kw["scale"] = scale
    if accum_out is not None:
        kw["accum_out"] = accum_out
    return P.step("act", lambda q: q.activation(out=out, in_=in_, func=func, **kw),
                  reads, writes, extra, 1)


def COPY(P, out, in_, reads=(), writes=(), extra=(), eng="dve"):
    return P.step(eng, lambda q: q.tensor_copy(out, in_), reads, writes, extra, 1)


def REDMAX(P, out, in_, reads=(), writes=(), extra=()):
    return P.step("dve", lambda q: q.reduce_max(out, in_, axis=AXX), reads, writes, extra, 1)


def REDSUM(P, out, in_, reads=(), writes=(), extra=()):
    return P.step("dve", lambda q: q.reduce_sum(out, in_, axis=AXX), reads, writes, extra, 1)


def RECIP(P, out, in_, reads=(), writes=(), extra=()):
    return P.step("dve", lambda q: q.reciprocal(out, in_), reads, writes, extra, 1)


def MEMSET(P, ap, val, reads=(), writes=(), extra=(), eng="pool"):
    return P.step(eng, lambda q: q.memset(ap, val), reads, writes, extra, 1)


def CCOMP(P, kind, op, ins, outs, groups, reads=(), writes=(), extra=()):
    def f(q):
        return q.collective_compute(kind, op, ins=ins, outs=outs, replica_groups=groups)
    waits = list(extra)
    for k in reads:
        w = P.resw.get(k)
        if w:
            waits.append(w)
    for k in writes:
        w = P.resw.get(k)
        if w:
            waits.append(w)
        waits.extend(P.resr.get(k, ()))
    P.dma_cnt["cc"] = P.dma_cnt.get("cc", 0)
    ccv = P.cnt.get("ccn", 0) + 1
    P.cnt["ccn"] = ccv
    P._record("pool", f, waits, "cc", 1)
    tok = ("cc", ccv)
    for k in reads:
        P.resr.setdefault(k, []).append(tok)
    for k in writes:
        P.resw[k] = tok
        P.resr[k] = []
    return tok


def FIN(P, scratch):
    MEMSET(P, scratch, 0.0, extra=P.all_toks(), eng="pool")



def stg_keys(p, cols):
    j0 = (p * 1024) // SLOT
    j1 = (p * 1024 + cols + SLOT - 1) // SLOT
    return [("stg", j) for j in range(j0, j1)]


def cast_resident(P, io_src, dst_bf, wst, keypfx, nchunks, cols, dstcols=None, key2=None):
    dstcols = dstcols if dstcols is not None else cols
    for dc in range(nchunks):
        p = dc % 2
        st = wst[:, p * 1024:p * 1024 + cols]
        DMA(P, "syn", st, io_src[dc * 128:(dc + 1) * 128, :], writes=stg_keys(p, cols))
        COPY(P, dst_bf[:, dc * dstcols:dc * dstcols + cols], st,
             reads=stg_keys(p, cols), writes=[(keypfx, dc) if key2 is None else (keypfx, dc, key2)],
             eng="dve" if dc % 2 == 0 else "pool")


def ln_stats(P, sb, ps, chunks_ap, chunk_keys, nch, eps, tag, gamma=False):
    for i in range(nch):
        MM(P, ps["ps_x"][0:1, :], sb["ones128"][:], chunks_ap(i), i == 0, i == nch - 1,
           reads=[("ones128",), chunk_keys(i)], writes=[("ps_x",)], incall=True)
    COPY(P, sb["lnr"][0:1, 0 * SLOT:1 * SLOT], ps["ps_x"][0:1, :], reads=[("ps_x",)], writes=[("lnr", 0)])
    for i in range(nch):
        ACTV(P, sb["sq"][:], chunks_ap(i), AF.Square, reads=[chunk_keys(i)], writes=[("sq",)])
        MM(P, ps["ps_x"][0:1, :], sb["ones128"][:], sb["sq"][:], i == 0, i == nch - 1,
           reads=[("ones128",), ("sq",)], writes=[("ps_x",)], incall=True)
    COPY(P, sb["lnr"][0:1, 1 * SLOT:2 * SLOT], ps["ps_x"][0:1, :], reads=[("ps_x",)], writes=[("lnr", 1)])
    TS(P, sb["lnr"][0:1, 2 * SLOT:3 * SLOT], sb["lnr"][0:1, 0 * SLOT:1 * SLOT], 1.0 / 2048.0, None, OP.mult,
       reads=[("lnr", 0)], writes=[("lnr", 2)])
    TT(P, sb["lnr"][0:1, 3 * SLOT:4 * SLOT], sb["lnr"][0:1, 2 * SLOT:3 * SLOT], sb["lnr"][0:1, 2 * SLOT:3 * SLOT], OP.mult,
       reads=[("lnr", 2)], writes=[("lnr", 3)])
    STT(P, sb["lnr"][0:1, 3 * SLOT:4 * SLOT], sb["lnr"][0:1, 1 * SLOT:2 * SLOT], 1.0 / 2048.0, sb["lnr"][0:1, 3 * SLOT:4 * SLOT], OP.mult, OP.subtract,
        reads=[("lnr", 1), ("lnr", 3)], writes=[("lnr", 3)])
    MEMSET(P, sb["epst"][:], eps, writes=[("epst",)])
    ACTV(P, sb["lnr"][0:1, 4 * SLOT:5 * SLOT], sb["lnr"][0:1, 3 * SLOT:4 * SLOT], AF.Sqrt,
         bias=sb["epst"][0:1, 0:1],
         reads=[("lnr", 3), ("epst",)], writes=[("lnr", 4)])
    RECIP(P, sb["lnr"][0:1, 5 * SLOT:6 * SLOT], sb["lnr"][0:1, 4 * SLOT:5 * SLOT], reads=[("lnr", 4)], writes=[("lnr", 5)])
    TT(P, sb["lnr"][0:1, 6 * SLOT:7 * SLOT], sb["lnr"][0:1, 2 * SLOT:3 * SLOT], sb["lnr"][0:1, 5 * SLOT:6 * SLOT], OP.mult,
       reads=[("lnr", 2), ("lnr", 5)], writes=[("lnr", 6)])
    TS(P, sb["lnr"][0:1, 6 * SLOT:7 * SLOT], sb["lnr"][0:1, 6 * SLOT:7 * SLOT], -1.0, None, OP.mult,
       reads=[("lnr", 6)], writes=[("lnr", 6)])
    MM(P, ps["ps_x"][:, 0:SLOT], sb["ones1r"][:], sb["lnr"][0:1, 5 * SLOT:6 * SLOT], True, True,
       reads=[("ones1r",), ("lnr", 5)], writes=[("ps_x",)])
    ACTV(P, sb["lna"][:], ps["ps_x"][:, 0:SLOT], AF.Copy, reads=[("ps_x",)], writes=[("lna",)])
    MM(P, ps["ps_x"][:, 0:SLOT], sb["ones1r"][:], sb["lnr"][0:1, 6 * SLOT:7 * SLOT], True, True,
       reads=[("ones1r",), ("lnr", 6)], writes=[("ps_x",)])
    ACTV(P, sb["lnb"][:], ps["ps_x"][:, 0:SLOT], AF.Copy, reads=[("ps_x",)], writes=[("lnb",)])



def emit_l0a(P, nc, io, sb, ps):
    x0T = io["x0T"]; qT_hbm = io["qT_hbm"]; part0 = io["part0"]

    DMA(P, "syn", sb["jmat"][:], io["jmat"], writes=[("jmat",)])
    DMA(P, "syn", sb["mdiag"][:], io["mdiag"], writes=[("mdiag",)])
    DMA(P, "syn", sb["cg"][0:1, 0:8], io["cvec_in"], writes=[("cg", 0)])
    DMA(P, "syn", sb["cg"][0:1, 8:16], io["gvec_in"], writes=[("cg", 1)])
    DMA(P, "syn", sb["cg"][0:1, 16:24], io["tvec_in"], writes=[("cg", 2)])
    MEMSET(P, sb["ones128"][:], 1.0, writes=[("ones128",)])
    MEMSET(P, sb["ones1r"][:], 1.0, writes=[("ones1r",)])
    MEMSET(P, sb["mones1r"][:], -1.0, writes=[("mones1r",)])

    u = sb["u"]; cg = sb["cg"]
    ACTV(P, u[0:1, 0:8], cg[0:1, 16:24], AF.Abs, reads=[("cg", 2)], writes=[("u",)])
    ACTV(P, u[0:1, 8:16], u[0:1, 0:8], AF.Exp, scale=-1.0, reads=[("u",)], writes=[("u",)])
    ACTV(P, u[0:1, 16:24], u[0:1, 8:16], AF.Ln, bias=1.0, reads=[("u",)], writes=[("u",)])
    TS(P, u[0:1, 24:32], cg[0:1, 16:24], 0.0, None, OP.max, reads=[("cg", 2)], writes=[("u",)])
    TT(P, u[0:1, 32:40], u[0:1, 16:24], u[0:1, 24:32], OP.add, reads=[("u",)], writes=[("u",)])
    TS(P, u[0:1, 32:40], u[0:1, 32:40], 1e-6, None, OP.max, reads=[("u",)], writes=[("u",)])
    TS(P, u[0:1, 40:48], cg[0:1, 0:8], 1.0, -1.0, OP.min, OP.max, reads=[("cg", 0)], writes=[("u",)])
    STT(P, cg[0:1, 0:8], u[0:1, 40:48], 1.0 / 4096.0, u[0:1, 32:40], OP.mult, OP.add,
        reads=[("u",)], writes=[("cg", 0)])
    ACTV(P, cg[0:1, 8:16], cg[0:1, 8:16], AF.Sigmoid, reads=[("cg", 1)], writes=[("cg", 1)])
    TS(P, cg[0:1, 16:24], cg[0:1, 0:8], 0.0, None, OP.max, reads=[("cg", 0)], writes=[("cg", 2)])
    MM(P, ps["ps_x"][:, 0:24], sb["ones1r"][:], cg[0:1, 0:24], True, True,
       reads=[("ones1r",), ("cg", 0), ("cg", 1), ("cg", 2)], writes=[("ps_x",)])
    ACTV(P, sb["Cbc"][:], ps["ps_x"][:, 0:24], AF.Copy, reads=[("ps_x",)], writes=[("Cbc",)])
    for h in range(8):
        TS(P, sb["BV"][:, h * 32:(h + 1) * 32], sb["jmat"][:], sb["Cbc"][:, h:h + 1], None, OP.mult,
           reads=[("jmat",), ("Cbc",)], writes=[("BV", h)])

    wst = sb["wstage"]
    cast_resident(P, io["wq_sl"], sb["bigw"], wst, "bigw", NDC, 1024)
    cast_resident(P, io["wk_sl"], sb["wk_bf"], wst, "wk", NDC, 256)
    cast_resident(P, io["wv_sl"], sb["wv_bf"], wst, "wv", NDC, 256)
    for dc in range(NDC):
        p = dc % 2
        DMA(P, "syn", wst[:, p * 1024:p * 1024 + 32], io["wa"][dc * 128:(dc + 1) * 128, :], writes=[("stg", 2 * p)])
        DMA(P, "syn", wst[:, p * 1024 + 32:p * 1024 + 64], io["wb"][dc * 128:(dc + 1) * 128, :], writes=[("stg2", 2 * p)])
        COPY(P, sb["wab_bf"][:, dc * 64:(dc + 1) * 64], wst[:, p * 1024:p * 1024 + 64],
             reads=[("stg", 2 * p), ("stg2", 2 * p)], writes=[("wab", dc)])

    for s in range(NSLOT):
        tsl = slice(s * SLOT, (s + 1) * SLOT)
        for dc in range(NDC):
            j = dc % 4
            stg = sb["wstage"][:, j * SLOT:(j + 1) * SLOT]
            DMA(P, "syn", stg, x0T[dc * 128:(dc + 1) * 128, tsl], writes=[("stg", j)])
            COPY(P, sb["xb"][:, dc * SLOT:(dc + 1) * SLOT], stg,
                 reads=[("stg", j)], writes=[("xb", dc)], eng="dve" if dc % 2 == 0 else "pool")
        for i, tab in enumerate(["qcos", "qsin", "kcos", "ksin"]):
            DMA(P, "syn", sb["tabs"][:, i * SLOT:(i + 1) * SLOT], io[tab][:, tsl], writes=[("tab", i)])

        for h in range(8):
            par = h % 2
            pw = ps["ps_w0"] if par == 0 else ps["ps_w1"]
            pwk = ("ps_w", par)
            for dc in range(NDC):
                MM(P, pw[:], sb["bigw"][:, dc * 1024 + h * 128:dc * 1024 + (h + 1) * 128],
                   sb["xb"][:, dc * SLOT:(dc + 1) * SLOT], dc == 0, dc == NDC - 1,
                   reads=[("bigw", dc), ("xb", dc)], writes=[pwk])
            qo = sb["qout"][:, par * SLOT:(par + 1) * SLOT]
            rt = sb["rtmp"]
            qc, qs_ = sb["tabs"][:, 0:SLOT], sb["tabs"][:, SLOT:2 * SLOT]
            TT(P, rt[:, 0:SLOT], pw[0:64, :], qc, OP.mult, reads=[pwk, ("tab", 0)], writes=[("rt", 0)])
            TT(P, rt[:, SLOT:2 * SLOT], pw[64:128, :], qs_, OP.mult, reads=[pwk, ("tab", 1)], writes=[("rt", 1)])
            TT(P, qo[0:64, :], rt[:, 0:SLOT], rt[:, SLOT:2 * SLOT], OP.subtract,
               reads=[("rt", 0), ("rt", 1)], writes=[("qout", par, 0)])
            TT(P, rt[:, 0:SLOT], pw[64:128, :], qc, OP.mult, reads=[pwk, ("tab", 0)], writes=[("rt", 0)])
            TT(P, rt[:, SLOT:2 * SLOT], pw[0:64, :], qs_, OP.mult, reads=[pwk, ("tab", 1)], writes=[("rt", 1)])
            TT(P, qo[64:128, :], rt[:, 0:SLOT], rt[:, SLOT:2 * SLOT], OP.add,
               reads=[("rt", 0), ("rt", 1)], writes=[("qout", par, 1)])
            DMA(P, "syn", qT_hbm[h * 128:(h + 1) * 128, tsl], qo,
                reads=[("qout", par, 0), ("qout", par, 1)], writes=[("qhbm", h, s)])

        for kv in range(2):
            pk = ps["ps_m0"] if kv == 0 else ps["ps_m1"]
            pkk = ("ps_m", kv)
            for dc in range(NDC):
                MM(P, pk[:], sb["wk_bf"][:, dc * 256 + kv * 128:dc * 256 + (kv + 1) * 128],
                   sb["xb"][:, dc * SLOT:(dc + 1) * SLOT], dc == 0, dc == NDC - 1,
                   reads=[("wk", dc), ("xb", dc)], writes=[pkk])
            kc_t, ks_t = sb["tabs"][:, 2 * SLOT:3 * SLOT], sb["tabs"][:, 3 * SLOT:4 * SLOT]
            rt = sb["rtmp"]
            TT(P, rt[:, 0:SLOT], pk[0:64, :], kc_t, OP.mult, reads=[pkk, ("tab", 2)], writes=[("rt", 0)])
            TT(P, rt[:, SLOT:2 * SLOT], pk[64:128, :], ks_t, OP.mult, reads=[pkk, ("tab", 3)], writes=[("rt", 1)])
            TT(P, sb["K"][0:64, kv * T + s * SLOT:kv * T + (s + 1) * SLOT],
               rt[:, 0:SLOT], rt[:, SLOT:2 * SLOT], OP.subtract,
               reads=[("rt", 0), ("rt", 1)], writes=[("K", kv, s)])
            TT(P, rt[:, 0:SLOT], pk[64:128, :], kc_t, OP.mult, reads=[pkk, ("tab", 2)], writes=[("rt", 0)])
            TT(P, rt[:, SLOT:2 * SLOT], pk[0:64, :], ks_t, OP.mult, reads=[pkk, ("tab", 3)], writes=[("rt", 1)])
            TT(P, sb["K"][64:128, kv * T + s * SLOT:kv * T + (s + 1) * SLOT],
               rt[:, 0:SLOT], rt[:, SLOT:2 * SLOT], OP.add,
               reads=[("rt", 0), ("rt", 1)], writes=[("K", kv, s, 1)])

        for tcl in range(4):
            kc = 4 * s + tcl
            par = tcl % 2
            pv_ = ps["ps_w0"] if par == 0 else ps["ps_w1"]
            pvk = ("ps_w", par)
            for dc in range(NDC):
                MM(P, pv_[:, 0:256], sb["xb"][:, dc * SLOT + tcl * 128:dc * SLOT + (tcl + 1) * 128],
                   sb["wv_bf"][:, dc * 256:(dc + 1) * 256], dc == 0, dc == NDC - 1,
                   reads=[("wv", dc), ("xb", dc)], writes=[pvk])
            COPY(P, sb["V"][:, kc * 256:(kc + 1) * 256], pv_[:, 0:256], reads=[pvk], writes=[("V", kc)])

        for which, colo in (("A", 0), ("B", 32)):
            pr = ps["ps_x"]
            for dc in range(NDC):
                MM(P, pr[0:32, :], sb["wab_bf"][:, dc * 64 + colo:dc * 64 + colo + 32],
                   sb["xb"][:, dc * SLOT:(dc + 1) * SLOT], dc == 0, dc == NDC - 1,
                   reads=[("wab", dc), ("xb", dc)], writes=[("ps_x",)])
            dst = (sb["AT"] if which == "A" else sb["BT"])[0:32, tsl]
            COPY(P, dst, pr[0:32, :], reads=[("ps_x",)], writes=[(which, s)])

    for hc in range(8):
        for half in range(2):
            i = 2 * hc + half
            p = i % 2
            st = wst[:, p * 1024:(p + 1) * 1024]
            DMA(P, "syn", st, io["wo_sl"][hc * 128:(hc + 1) * 128, half * 1024:(half + 1) * 1024],
                writes=stg_keys(p, 1024))
            COPY(P, sb["bigw"][:, i * 1024:(i + 1) * 1024], st,
                 reads=stg_keys(p, 1024), writes=[("bigw", i)])

    P.phase_fence()
    sb = dict(sb)
    sb["sq"] = sb["xb"][:, 4 * SLOT:6 * SLOT].bitcast(F32)
    sb["Ub"] = sb["xb"][:, 6 * SLOT:8 * SLOT].bitcast(F32)
    for kv in range(2):
        for cc in range(8):
            ACTV(P, sb["sq"], sb["K"][:, kv * T + cc * SLOT:kv * T + (cc + 1) * SLOT], AF.Square,
                 reads=[("K", kv, cc), ("K", kv, cc, 1)], writes=[("sq",)])
            MM(P, ps["ps_x"][0:1, :], sb["ones128"][:], sb["sq"], True, True,
               reads=[("ones128",), ("sq",)], writes=[("ps_x",)])
            REDMAX(P, sb["k2"][0:1, cc:cc + 1], ps["ps_x"][0:1, 0:SLOT],
                   reads=[("ps_x",)], writes=[("k2", cc)])
        REDMAX(P, u[0:1, 48 + kv:49 + kv], sb["k2"][0:1, 0:8],
               reads=[("k2", cc2) for cc2 in range(8)], writes=[("u",)])
        ACTV(P, u[0:1, 50 + kv:51 + kv], u[0:1, 48 + kv:49 + kv], AF.Ln,
             reads=[("u",)], writes=[("umk", kv)])

    for s in range(NSLOT):
        tsl = slice(s * SLOT, (s + 1) * SLOT)
        nkc = 4 * s + 4
        DMA(P, "syn", sb["qidxr"][0:1, :], io["qidx"][0:1, tsl], writes=[("qidxr",)])
        for h in range(8):
            DMA(P, "syn", sb["Qatt"][:, h * SLOT:(h + 1) * SLOT], qT_hbm[h * 128:(h + 1) * 128, tsl],
                reads=[("qhbm", h, s)], writes=[("Qatt", h)])

        for kc in range(nkc):
            pm = ps["ps_m0"] if kc % 2 == 0 else ps["ps_m1"]
            pmk = ("ps_m", kc % 2)
            MM(P, pm[:], sb["BT"][0:32, kc * 128:(kc + 1) * 128], sb["AT"][0:32, tsl], True, True,
               reads=[("B", kc // 4), ("A", s)], writes=[pmk])
            ACTV(P, sb["Mt"][:, kc * SLOT:(kc + 1) * SLOT], pm[:], AF.Sigmoid,
                 reads=[pmk], writes=[("Mt", kc)])
        QKPS = ["ps_qk0", "ps_qk1", "ps_w0", "ps_w1"]
        QKK = [("ps_qk", 0), ("ps_qk", 1), ("ps_w", 0), ("ps_w", 1)]
        tscv = [sb["tsc2"][:, 0:SLOT], sb["tsc2"][:, SLOT:2 * SLOT],
                sb["xb"][:, 0:2 * SLOT].bitcast(F32), sb["xb"][:, 2 * SLOT:4 * SLOT].bitcast(F32)]
        fin_pending = []

        def finalize_head(hf):
            pf = hf % 2
            MM(P, ps["ps_x"][:, 0:SLOT], sb["ones1r"][:], sb["rl"][0:1, pf * SLOT:pf * SLOT + SLOT],
               True, True, reads=[("ones1r",), ("rl", pf)], writes=[("ps_x",)])
            ACTV(P, sb["rb"][:, pf * SLOT:(pf + 1) * SLOT], ps["ps_x"][:, 0:SLOT], AF.Copy,
                 reads=[("ps_x",)], writes=[("rb", pf)])
            TT(P, sb["AO"][:, hf * SLOT:(hf + 1) * SLOT], sb["AO"][:, hf * SLOT:(hf + 1) * SLOT],
               sb["rb"][:, pf * SLOT:(pf + 1) * SLOT], OP.mult,
               reads=[("rb", pf), ("AO", hf)], writes=[("AO", hf)])

        for h in range(8):
            kv = h // 4
            ACTV(P, sb["sq"], sb["Qatt"][:, h * SLOT:(h + 1) * SLOT], AF.Square,
                 reads=[("Qatt", h)], writes=[("sq",)])
            MM(P, ps["ps_x"][0:1, :], sb["ones128"][:], sb["sq"], True, True,
               reads=[("ones128",), ("sq",)], writes=[("ps_x",)])
            REDMAX(P, u[0:1, 52:53], ps["ps_x"][0:1, 0:SLOT], reads=[("ps_x",)], writes=[("u",)])
            ACTV(P, u[0:1, 53:54], u[0:1, 52:53], AF.Ln, reads=[("u",)], writes=[("u",)])
            TT(P, u[0:1, 54:55], u[0:1, 53:54], u[0:1, 50 + kv:51 + kv], OP.add,
               reads=[("u",), ("umk", kv)], writes=[("u",)])
            ACTV(P, u[0:1, 56:57], u[0:1, 54:55], AF.Exp, scale=0.5, reads=[("u",)], writes=[("u",)])
            TS(P, u[0:1, 56:57], u[0:1, 56:57], 2.0, None, OP.add, reads=[("u",)], writes=[("u",)])
            TS(P, sb["erow"][0:1, :], sb["qidxr"][0:1, :], cg[0:1, h:h + 1], u[0:1, 56:57],
               OP.mult, OP.add, reads=[("qidxr",), ("cg", 0), ("u",)], writes=[("erow",)])
            def front(kc):
                par = kc % 4
                pq = ps[QKPS[par]]
                pqk = QKK[par]
                MM(P, pq[:], sb["K"][:, kv * T + kc * 128:kv * T + (kc + 1) * 128],
                   sb["Qatt"][:, h * SLOT:(h + 1) * SLOT], True, False,
                   reads=[("K", kv, kc // 4), ("K", kv, kc // 4, 1), ("Qatt", h)], writes=[pqk])
                MM(P, pq[:], sb["mones1r"][:], sb["erow"][0:1, :], False, True,
                   reads=[("mones1r",), ("erow",)], writes=[pqk])
                tdst = tscv[par]
                STT(P, tdst, sb["Mt"][:, kc * SLOT:(kc + 1) * SLOT], sb["Cbc"][:, 8 + h:9 + h], pq[:],
                    OP.mult, OP.add, reads=[("Mt", kc), ("Cbc",), pqk], writes=[("tsc", par)])
                if kc >= 4 * s:
                    rr = kc - 4 * s
                    TT(P, tdst, tdst, sb["mdiag"][:, rr * SLOT:(rr + 1) * SLOT], OP.add,
                       reads=[("mdiag",), ("tsc", par)], writes=[("tsc", par)])
                ACTV(P, sb["Pp"][:, (kc % 2) * SLOT:(kc % 2 + 1) * SLOT], tdst, AF.Exp,
                     bias=sb["BV"][:, h * 32 + kc:h * 32 + kc + 1],
                     reads=[("tsc", par), ("BV", h)], writes=[("Pp", kc % 2)])

            def back(kc):
                MM(P, ps["ps_pv"][:], sb["V"][:, kc * 256 + kv * 128:kc * 256 + (kv + 1) * 128],
                   sb["Pp"][:, (kc % 2) * SLOT:(kc % 2 + 1) * SLOT], kc == 0, kc == nkc - 1,
                   reads=[("Pp", kc % 2), ("V", kc)], writes=[("ps_pv",)])
                if kc == 0:
                    COPY(P, sb["S"][:], sb["Pp"][:, (kc % 2) * SLOT:(kc % 2 + 1) * SLOT],
                         reads=[("Pp", kc % 2)], writes=[("S",)], eng="pool")
                else:
                    TT(P, sb["S"][:], sb["S"][:], sb["Pp"][:, (kc % 2) * SLOT:(kc % 2 + 1) * SLOT], OP.add,
                       reads=[("Pp", kc % 2), ("S",)], writes=[("S",)], eng="pool")

            if h > 0:
                fin_pending.append(h - 1)
            for kc in range(nkc):
                if kc == 2 and fin_pending:
                    finalize_head(fin_pending.pop(0))
                if kc >= 2:
                    back(kc - 2)
                front(kc)
            back(max(nkc - 2, 0))
            if nkc >= 2:
                back(nkc - 1)
            MM(P, ps["ps_x"][0:1, :], sb["ones128"][:], sb["S"][:], True, True,
               reads=[("ones128",), ("S",)], writes=[("ps_x",)])
            RECIP(P, sb["rl"][0:1, (h % 2) * SLOT:(h % 2) * SLOT + SLOT], ps["ps_x"][0:1, 0:SLOT],
                  reads=[("ps_x",)], writes=[("rl", h % 2)])
            COPY(P, sb["AO"][:, h * SLOT:(h + 1) * SLOT], ps["ps_pv"][:],
                 reads=[("ps_pv",)], writes=[("AO", h)])
        finalize_head(7)
        for dc2 in range(NDC):
            par = dc2 % 2
            pw = ps["ps_w0"] if par == 0 else ps["ps_w1"]
            pwk = ("ps_w", par)
            for hc in range(8):
                MM(P, pw[:], sb["bigw"][:, hc * D + dc2 * 128:hc * D + (dc2 + 1) * 128],
                   sb["AO"][:, hc * SLOT:(hc + 1) * SLOT], hc == 0, hc == 7,
                   reads=[("AO", hc), ("bigw", 2 * hc + dc2 // 8)], writes=[pwk])
            ACTV(P, sb["po"][:, par * SLOT:(par + 1) * SLOT], pw[:], AF.Copy,
                 reads=[pwk], writes=[("po", par)])
            DMA(P, "pool", part0[dc2 * 128:(dc2 + 1) * 128, tsl], sb["po"][:, par * SLOT:(par + 1) * SLOT],
                reads=[("po", par)], writes=[("part", s, dc2)])

    FIN(P, u[0:1, 63:64])


def l0a_sbuf(ctx_sb):
    sb = {}
    add = ctx_sb
    sb["K"] = add("K", [128, 2 * T], BF16)
    sb["V"] = add("V", [128, NKC * 256], BF16)
    sb["AT"] = add("AT", [32, T], BF16)
    sb["BT"] = add("BT", [32, T], BF16)
    sb["jmat"] = add("jmat", [128, 32], F32)
    sb["mdiag"] = add("mdiag", [128, 4 * SLOT], BF16)
    sb["ones128"] = add("ones128", [128, 1], F32)
    sb["ones1r"] = add("ones1r", [1, 128], F32)
    sb["cg"] = add("cg", [1, 64], F32)
    sb["Cbc"] = add("Cbc", [128, 24], F32)
    sb["u"] = add("u", [1, 64], F32)
    sb["wstage"] = add("wstage", [128, 2048], F32)
    sb["bigw"] = add("bigw", [128, 16384], BF16)
    sb["wk_bf"] = add("wk_bf", [128, 4096], BF16)
    sb["wv_bf"] = add("wv_bf", [128, 4096], BF16)
    sb["wab_bf"] = add("wab_bf", [128, 1024], BF16)
    sb["xb"] = add("xb", [128, NDC * SLOT], BF16)
    sb["tabs"] = add("tabs", [64, 4 * SLOT], BF16)
    sb["rtmp"] = add("rtmp", [64, 2 * SLOT], F32)
    sb["qout"] = add("qout", [128, 2 * SLOT], BF16)
    sb["Qatt"] = add("Qatt", [128, 8 * SLOT], BF16)
    sb["Mt"] = add("Mt", [128, 32 * SLOT], BF16)
    sb["S"] = add("S", [128, SLOT], F32)
    sb["Pp"] = add("Pp", [128, 2 * SLOT], BF16)
    sb["tsc2"] = add("tsc2", [128, 2 * SLOT], F32)
    sb["AO"] = add("AO", [128, 8 * SLOT], BF16)
    sb["BV"] = add("BV", [128, 8 * 32], F32)
    sb["rl"] = add("rl", [1, 2 * SLOT], F32)
    sb["erow"] = add("erow", [1, SLOT], BF16)
    sb["mones1r"] = add("mones1r", [1, 128], BF16)
    sb["rb"] = add("rb", [128, 2 * SLOT], BF16)
    sb["po"] = add("po", [128, 2 * SLOT], BF16)
    sb["k2"] = add("k2", [1, 64], F32)
    sb["qidxr"] = add("qidxr", [1, SLOT], F32)
    return sb


PSUM_NAMES = ["ps_qk0", "ps_qk1", "ps_pv", "ps_m0", "ps_m1", "ps_w0", "ps_w1", "ps_x"]


def build_phase(emit_fn, io_spec, sbuf_fn, internal_spec=()):
    nc = bass.Bass("TRN2", target_bir_lowering=False)
    io = {}
    for (name, shape, dt, kind) in io_spec:
        io[name] = nc.dram_tensor(name, list(shape), dt, kind=kind).ap()
    for (name, shape, dt) in internal_spec:
        io[name] = nc.dram_tensor(name, list(shape), dt).ap()
    from contextlib import ExitStack
    with ExitStack() as stack:
        sbadd = lambda n, s, d: stack.enter_context(nc.sbuf_tensor("sb_" + n, s, d))
        sb = sbuf_fn(sbadd)
        ps = {n: stack.enter_context(nc.psum_tensor(n, [128, 512], F32)) for n in PSUM_NAMES}
        Pr0 = Prog()
        sems = {n: stack.enter_context(nc.semaphore("sem_" + n)) for n in Pr0.sem_names()}
        block = stack.enter_context(nc.Block())
        Pr = Prog()
        emit_fn(Pr, nc, io, sb, ps)
        replay(Pr, block, sems)
    return nc


L0A_IO = [
    ("x0T", [D, T], F32, "ExternalInput"),
    ("wq_sl", [D, 1024], F32, "ExternalInput"),
    ("wk_sl", [D, 256], F32, "ExternalInput"),
    ("wv_sl", [D, 256], F32, "ExternalInput"),
    ("wo_sl", [1024, D], F32, "ExternalInput"),
    ("wa", [D, 32], F32, "ExternalInput"),
    ("wb", [D, 32], F32, "ExternalInput"),
    ("cvec_in", [1, 8], F32, "ExternalInput"),
    ("gvec_in", [1, 8], F32, "ExternalInput"),
    ("tvec_in", [1, 8], F32, "ExternalInput"),
    ("qcos", [64, T], BF16, "ExternalInput"),
    ("qsin", [64, T], BF16, "ExternalInput"),
    ("kcos", [64, T], BF16, "ExternalInput"),
    ("ksin", [64, T], BF16, "ExternalInput"),
    ("jmat", [128, 32], F32, "ExternalInput"),
    ("qidx", [1, T], F32, "ExternalInput"),
    ("mdiag", [128, 4 * SLOT], BF16, "ExternalInput"),
    ("part0", [D, T], BF16, "ExternalOutput"),
]
L0A_INT = [("qT_hbm", [1024, T], BF16)]


def build_l0a():
    return build_phase(emit_l0a, L0A_IO, l0a_sbuf, L0A_INT)



def rope_tables(d_head, half, scale):
    import ml_dtypes
    inv = 1.0 / (10000.0 ** (np.arange(0, d_head, 2, dtype=np.float32) / d_head))
    ang = np.arange(T, dtype=np.float32)[:, None] * inv[None, :]
    return (np.cos(ang).T * scale).astype(ml_dtypes.bfloat16), (np.sin(ang).T * scale).astype(ml_dtypes.bfloat16)


def host_jmat():
    p = np.arange(128, dtype=np.float32)[:, None]
    kcv = np.arange(32, dtype=np.float32)[None, :]
    return (kcv * 128.0 + p).astype(np.float32)


def host_mdiag():
    import ml_dtypes
    out = np.zeros((128, 4 * SLOT), dtype=np.float32)
    p = np.arange(128)[:, None]
    f = np.arange(SLOT)[None, :]
    for r in range(4):
        out[:, r * SLOT:(r + 1) * SLOT] = np.where(f >= r * 128 + p, 0.0, NEG)
    return out.astype(ml_dtypes.bfloat16)


def l0a_in_maps(inputs):
    x = np.asarray(inputs["L0_attn_x"], dtype=np.float32)
    Wq = np.asarray(inputs["L0_attn_Wq"], np.float32)
    Wk = np.asarray(inputs["L0_attn_Wk"], np.float32)
    Wv = np.asarray(inputs["L0_attn_Wv"], np.float32)
    Wo = np.asarray(inputs["L0_attn_Wo"], np.float32)
    WA = np.asarray(inputs["L0_attn_pol_WA"], np.float32)
    WB = np.asarray(inputs["L0_attn_pol_WB"], np.float32)
    pd = np.asarray(inputs["L0_attn_pol_dir"], np.float32)
    pg = np.asarray(inputs["L0_attn_pol_gate"], np.float32)
    tg = np.asarray(inputs["L0_attn_gtp_gamma"], np.float32)
    qc, qs = rope_tables(128, 64, 1.0 / math.sqrt(128.0))
    kc, ks = rope_tables(128, 64, 1.0)
    jm = host_jmat()
    md = host_mdiag()
    maps = []
    for c in range(8):
        b, H = c // 2, c % 2
        xT = np.ascontiguousarray(x[b].T)
        maps.append({
            "x0T": xT,
            "wq_sl": np.ascontiguousarray(Wq[:, H * 1024:(H + 1) * 1024]),
            "wk_sl": np.ascontiguousarray(Wk[:, H * 256:(H + 1) * 256]),
            "wv_sl": np.ascontiguousarray(Wv[:, H * 256:(H + 1) * 256]),
            "wo_sl": np.ascontiguousarray(Wo[H * 1024:(H + 1) * 1024, :]),
            "wa": WA, "wb": WB,
            "cvec_in": pd[H * 8:(H + 1) * 8].reshape(1, 8),
            "gvec_in": pg[H * 8:(H + 1) * 8].reshape(1, 8),
            "tvec_in": tg[H * 8:(H + 1) * 8].reshape(1, 8),
            "qcos": qc, "qsin": qs, "kcos": kc, "ksin": ks,
            "jmat": jm, "mdiag": md, "qidx": np.arange(T, dtype=np.float32).reshape(1, T),
        })
    return maps





def emit_fin(P, nc, io, sb, ps, with_xm=False, xm_eps=1e-5):
    MEMSET(P, sb["ones128"][:], 1.0, writes=[("ones128",)])
    MEMSET(P, sb["ones1r"][:], 1.0, writes=[("ones1r",)])
    fused = "par_sum" in io
    if fused:
        DMA(P, "syn", sb["ind2"][0:1, :], io["ind2"], writes=[("ind2",)])
        MM(P, ps["ps_x"][:, 0:2], sb["ones1r"][:], sb["ind2"][0:1, :], True, True,
           reads=[("ones1r",), ("ind2",)], writes=[("ps_x",)])
        ACTV(P, sb["i2b"][:], ps["ps_x"][:, 0:2], AF.Copy, reads=[("ps_x",)], writes=[("i2b",)])
    if with_xm:
        DMA(P, "syn", sb["lnw"][:], io["ln1_fm"], writes=[("lnw",)])
    for os_ in range(4):
        tsl = slice(os_ * SLOT, (os_ + 1) * SLOT)
        for dc in range(NDC):
            j = dc % 2
            pa_t = sb["pstg"][:, j * 2 * SLOT:j * 2 * SLOT + SLOT]
            pb_t = sb["pstg"][:, j * 2 * SLOT + SLOT:j * 2 * SLOT + 2 * SLOT]
            xst = sb["xstg"][:, j * SLOT:(j + 1) * SLOT]
            if fused:
                par_ = io["par_sum"]
                DMA(P, "syn", pa_t, par_[dc * 128:(dc + 1) * 128, os_ * SLOT:(os_ + 1) * SLOT],
                    writes=[("pstg", j, 0)])
                DMA(P, "syn", pb_t, par_[dc * 128:(dc + 1) * 128, TC + os_ * SLOT:TC + (os_ + 1) * SLOT],
                    writes=[("pstg", j, 1)])
            else:
                DMA(P, "syn", pa_t, io["pa"][dc * 128:(dc + 1) * 128, tsl], writes=[("pstg", j, 0)])
                DMA(P, "syn", pb_t, io["pb"][dc * 128:(dc + 1) * 128, tsl], writes=[("pstg", j, 1)])
            DMA(P, "syn", xst, io["xprev"][dc * 128:(dc + 1) * 128, tsl], writes=[("xstg", j)])
            xs = sb["XS"][:, dc * SLOT:(dc + 1) * SLOT]
            if fused:
                TS(P, xs, pa_t, sb["i2b"][:, 0:1], None, OP.mult,
                   reads=[("pstg", j, 0), ("i2b",)], writes=[("XS", dc)])
                STT(P, xs, pb_t, sb["i2b"][:, 1:2], xs, OP.mult, OP.add,
                    reads=[("pstg", j, 1), ("i2b",), ("XS", dc)], writes=[("XS", dc)])
            else:
                TT(P, xs, pa_t, pb_t, OP.add, reads=[("pstg", j, 0), ("pstg", j, 1)], writes=[("XS", dc)])
            TT(P, xs, xs, xst, OP.add, reads=[("xstg", j), ("XS", dc)], writes=[("XS", dc)])
        ln_stats(P, sb, ps, lambda i: sb["XS"][:, i * SLOT:(i + 1) * SLOT],
                 lambda i: ("XS", i), NDC, 1e-5, "fin")
        for dc in range(NDC):
            xs = sb["XS"][:, dc * SLOT:(dc + 1) * SLOT]
            xo = sb["XO"][:, dc * SLOT:(dc + 1) * SLOT]
            TT(P, xo, xs, sb["lna"][:], OP.mult, reads=[("XS", dc), ("lna",)], writes=[("XO", dc)])
            TT(P, xo, xo, sb["lnb"][:], OP.add, reads=[("lnb",), ("XO", dc)], writes=[("XO", dc)])
            DMA(P, "pool", io["xout"][dc * 128:(dc + 1) * 128, tsl], xo,
                reads=[("XO", dc)], writes=[("xoutd", os_, dc)])
        if "xl_cc" in io and os_ == 3:
            for dc in range(NDC):
                COPY(P, sb["xlt"][:, dc:dc + 1], sb["XO"][:, dc * SLOT + SLOT - 1:dc * SLOT + SLOT],
                     reads=[("XO", dc)], writes=[("xlt",)])
            DMA(P, "pool", io["xl_cc"][:, :], sb["xlt"][:], reads=[("xlt",)], writes=[("xlcc",)])
        if with_xm:
            for i in range(NDC):
                ACTV(P, sb["sq"][:], sb["XO"][:, i * SLOT:(i + 1) * SLOT], AF.Square,
                     reads=[("XO", i)], writes=[("sq",)])
                MM(P, ps["ps_x"][0:1, :], sb["ones128"][:], sb["sq"][:], i == 0, i == NDC - 1,
                   reads=[("ones128",), ("sq",)], writes=[("ps_x",)], incall=True)
            TS(P, sb["lnr"][0:1, 0 * SLOT:1 * SLOT], ps["ps_x"][0:1, 0:SLOT], 1.0 / 2048.0, None, OP.mult,
               reads=[("ps_x",)], writes=[("lnr", 0)])
            MEMSET(P, sb["epst"][:], xm_eps, writes=[("epst",)])
            ACTV(P, sb["lnr"][0:1, 1 * SLOT:2 * SLOT], sb["lnr"][0:1, 0 * SLOT:1 * SLOT], AF.Sqrt,
                 bias=sb["epst"][0:1, 0:1],
                 reads=[("lnr", 0), ("epst",)], writes=[("lnr", 1)])
            RECIP(P, sb["lnr"][0:1, 2 * SLOT:3 * SLOT], sb["lnr"][0:1, 1 * SLOT:2 * SLOT], reads=[("lnr", 1)], writes=[("lnr", 2)])
            MM(P, ps["ps_x"][:, 0:SLOT], sb["ones1r"][:], sb["lnr"][0:1, 2 * SLOT:3 * SLOT], True, True,
               reads=[("ones1r",), ("lnr", 2)], writes=[("ps_x",)])
            ACTV(P, sb["lna"][:], ps["ps_x"][:, 0:SLOT], AF.Copy, reads=[("ps_x",)], writes=[("lna",)])
            for dc in range(NDC):
                xo = sb["XO"][:, dc * SLOT:(dc + 1) * SLOT]
                xm_t = sb["xmout"][:, (dc % 2) * SLOT:(dc % 2 + 1) * SLOT]
                TT(P, xm_t, xo, sb["lna"][:], OP.mult, reads=[("XO", dc), ("lna",)],
                   writes=[("xmout", dc % 2)])
                TS(P, xm_t, xm_t, sb["lnw"][:, dc:dc + 1], None, OP.mult,
                   reads=[("lnw",), ("xmout", dc % 2)], writes=[("xmout", dc % 2)])
                DMA(P, "pool", io["xm"][dc * 128:(dc + 1) * 128, tsl], xm_t,
                    reads=[("xmout", dc % 2)], writes=[("xmd", os_, dc)])
    FIN(P, sb["lnr"][0:1, 0:1])


def fin_sbuf(add, with_xm=False):
    sb = {}
    sb["ones128"] = add("ones128", [128, 1], F32)
    sb["ones1r"] = add("ones1r", [1, 128], F32)
    sb["pstg"] = add("pstg", [128, 4 * SLOT], BF16)
    sb["xstg"] = add("xstg", [128, 2 * SLOT], F32)
    sb["XS"] = add("XS", [128, NDC * SLOT], F32)
    sb["XO"] = add("XO", [128, NDC * SLOT], F32)
    sb["sq"] = add("sq", [128, SLOT], F32)
    sb["lnr"] = add("lnr", [1, 8 * SLOT], F32)
    sb["lna"] = add("lna", [128, SLOT], F32)
    sb["lnb"] = add("lnb", [128, SLOT], F32)
    sb["epst"] = add("epst", [128, 1], F32)
    sb["ind2"] = add("ind2", [1, 2], F32)
    sb["i2b"] = add("i2b", [128, 2], F32)
    sb["xlt"] = add("xlt", [128, NDC], F32)
    if with_xm:
        sb["lnw"] = add("lnw", [128, NDC], F32)
        sb["xmout"] = add("xmout", [128, 2 * SLOT], BF16)
    return sb


FIN_IO = [
    ("pa", [D, TC], BF16, "ExternalInput"),
    ("pb", [D, TC], BF16, "ExternalInput"),
    ("xprev", [D, TC], F32, "ExternalInput"),
    ("xout", [D, TC], F32, "ExternalOutput"),
]
FIN_XM_IO = FIN_IO + [
    ("ln1_fm", [128, NDC], F32, "ExternalInput"),
    ("xm", [D, TC], BF16, "ExternalOutput"),
]


def build_fin(with_xm=False, xm_eps=1e-5):
    return build_phase(lambda P, nc, io, sb, ps: emit_fin(P, nc, io, sb, ps, with_xm, xm_eps),
                       FIN_XM_IO if with_xm else FIN_IO,
                       lambda add: fin_sbuf(add, with_xm))



def fm_stats_scalar(P, sb, ps, fm_ap, fm_keys, eps, idx, sub_mean=True):
    u = sb["u"]
    MM(P, ps["ps_x"][0:1, 0:NDC], sb["ones128"][:], fm_ap, True, True,
       reads=[("ones128",)] + fm_keys, writes=[("ps_x",)])
    REDSUM(P, u[0:1, idx:idx + 1], ps["ps_x"][0:1, 0:NDC], reads=[("ps_x",)], writes=[("u",)])
    ACTV(P, sb["vsq"][:], fm_ap, AF.Square, reads=fm_keys, writes=[("vsq",)])
    MM(P, ps["ps_x"][0:1, 0:NDC], sb["ones128"][:], sb["vsq"][:], True, True,
       reads=[("ones128",), ("vsq",)], writes=[("ps_x",)])
    REDSUM(P, u[0:1, idx + 1:idx + 2], ps["ps_x"][0:1, 0:NDC], reads=[("ps_x",)], writes=[("u",)])
    TS(P, u[0:1, idx:idx + 1], u[0:1, idx:idx + 1], 1.0 / 2048.0, None, OP.mult,
       reads=[("u",)], writes=[("u",)])
    TT(P, u[0:1, idx + 2:idx + 3], u[0:1, idx:idx + 1], u[0:1, idx:idx + 1], OP.mult,
       reads=[("u",)], writes=[("u",)])
    STT(P, u[0:1, idx + 2:idx + 3], u[0:1, idx + 1:idx + 2], 1.0 / 2048.0, u[0:1, idx + 2:idx + 3],
        OP.mult, OP.subtract, reads=[("u",)], writes=[("u",)])
    MEMSET(P, sb["epst"][:], eps, writes=[("epst",)])
    ACTV(P, u[0:1, idx + 3:idx + 4], u[0:1, idx + 2:idx + 3], AF.Sqrt, bias=sb["epst"][0:1, 0:1],
         reads=[("u",), ("epst",)], writes=[("u",)])
    RECIP(P, u[0:1, idx + 1:idx + 2], u[0:1, idx + 3:idx + 4], reads=[("u",)], writes=[("u",)])
    MM(P, ps["ps_x"][:, 0:2], sb["ones1r"][:], u[0:1, idx:idx + 2], True, True,
       reads=[("ones1r",), ("u",)], writes=[("ps_x",)])
    ACTV(P, sb["vb"][:, 0:2], ps["ps_x"][:, 0:2], AF.Copy, reads=[("ps_x",)], writes=[("vb",)])


def emit_f0(P, nc, io, sb, ps):
    MEMSET(P, sb["ones128"][:], 1.0, writes=[("ones128",)])
    MEMSET(P, sb["ones1r"][:], 1.0, writes=[("ones1r",)])
    names = ["ing4", "inb4", "png4", "pnb4", "d1b4", "d2b4", "aggb4", "bcb4", "gb", "og", "ob", "ind"]
    if "xlg" not in io:
        names.append("xlast4")
    for name in names:
        DMA(P, "syn", sb[name][:], io[name], writes=[(name,)])
    if "xlg" in io:
        for b in range(4):
            DMA(P, "syn", sb["vtmp"][:], io["xlg"][(2 * b + 1) * 128:(2 * b + 2) * 128, :],
                writes=[("vtmp",)])
            COPY(P, sb["xlast4"].rearrange("p (f b) -> p b f", b=4)[:, b, :], sb["vtmp"][:],
                 reads=[("vtmp",)], writes=[("xlast4",)])
    WB = sb["WB"]
    for dc in range(NDC):
        DMA(P, "syn", WB[:, dc * 1024:(dc + 1) * 1024], io["d1_bf"][dc * 128:(dc + 1) * 128, :],
            writes=[("WB", dc)])
    for ic in range(8):
        DMA(P, "syn", WB[:, 16384 + ic * D:16384 + (ic + 1) * D], io["d2_bf"][ic * 128:(ic + 1) * 128, :],
            writes=[("WB", 16 + 2 * ic), ("WB", 16 + 2 * ic + 1)])

    u = sb["u"]
    th4 = sb["th4"]; tfm4 = sb["tfm4"]; tbf4 = sb["tbf4"]

    def bview(t, b, nch):
        return t.rearrange("p (f b) -> p b f", b=4)[:, b, :]

    def matvec4(src_bf, vec_bf, vec_key, out4, out_key, nchunks=NDC):
        for dci in range(nchunks):
            p = dci % 2
            wst = sb["wstg"][:, p * D:(p + 1) * D]
            DMA(P, "syn", wst, src_bf[dci * 128:(dci + 1) * 128, :], writes=[("wstg", p)])
            for dco in range(NDC):
                pm = ps["ps_m0"] if dco % 2 == 0 else ps["ps_m1"]
                pmk = ("ps_m", dco % 2)
                MM(P, pm[:, 0:4], wst[:, dco * 128:(dco + 1) * 128], vec_bf[:, dci * 4:(dci + 1) * 4],
                   True, True, reads=[("wstg", p), (vec_key,)], writes=[pmk])
                TT(P, out4[:, dco * 4:(dco + 1) * 4], out4[:, dco * 4:(dco + 1) * 4], pm[:, 0:4], OP.add,
                   reads=[pmk, (out_key,)], writes=[(out_key,)])

    def ln4(src4, src_key, dst4, dst_key, gamma, gamma_key, beta, beta_key, eps):
        for b in range(4):
            COPY(P, sb["vtmp"][:], bview(src4, b, NDC), reads=[(src_key,)], writes=[("vtmp",)])
            fm_stats_scalar(P, sb, ps, sb["vtmp"][:], [("vtmp",)], eps, 8)
            dv = bview(dst4, b, NDC)
            TS(P, dv, bview(src4, b, NDC), sb["vb"][:, 0:1], None, OP.subtract,
               reads=[(src_key,), ("vb",)], writes=[(dst_key,)])
            TS(P, dv, dv, sb["vb"][:, 1:2], None, OP.mult, reads=[("vb",), (dst_key,)], writes=[(dst_key,)])
        if gamma is not None:
            TT(P, dst4[:], dst4[:], gamma[:], OP.mult, reads=[(gamma_key,), (dst_key,)], writes=[(dst_key,)])
            TT(P, dst4[:], dst4[:], beta[:], OP.add, reads=[(beta_key,), (dst_key,)], writes=[(dst_key,)])

    ln4(sb["xlast4"], "xlast4", tfm4, "tfm4", sb["ing4"], "ing4", sb["inb4"], "inb4", 1e-6)
    COPY(P, tbf4[:], tfm4[:], reads=[("tfm4",)], writes=[("tbf4",)])
    MEMSET(P, th4[:], 0.0, writes=[("th4",)])
    matvec4(io["agg_bf"], tbf4, "tbf4", th4, "th4")
    TT(P, th4[:], th4[:], sb["aggb4"][:], OP.add, reads=[("aggb4",), ("th4",)], writes=[("th4",)])
    DMA(P, "pool", io["dbg_t0"][:, :], th4[:], reads=[("th4",)], writes=[("dbg0",)])

    groups = [list(range(8))]
    for step in range(4):
        ln4(th4, "th4", tfm4, "tfm4", sb["png4"], "png4", sb["pnb4"], "pnb4", 1e-6)
        COPY(P, tbf4[:], tfm4[:], reads=[("tfm4",)], writes=[("tbf4",)])
        for hc in range(8):
            pm = ps["ps_m0"] if hc % 2 == 0 else ps["ps_m1"]
            pmk = ("ps_m", hc % 2)
            for dci in range(NDC):
                MM(P, pm[:, 0:4], WB[:, dci * 1024 + hc * 128:dci * 1024 + (hc + 1) * 128],
                   tbf4[:, dci * 4:(dci + 1) * 4], dci == 0, dci == NDC - 1,
                   reads=[("WB", dci), ("tbf4",)], writes=[pmk])
            COPY(P, sb["h4"][:, hc * 4:(hc + 1) * 4], pm[:, 0:4], reads=[pmk], writes=[("h4",)])
        TT(P, sb["h4"][:], sb["h4"][:], sb["d1b4"][:], OP.add, reads=[("d1b4",), ("h4",)], writes=[("h4",)])
        ACTV(P, sb["g4"][:], sb["h4"][:], AF.Gelu, reads=[("h4",)], writes=[("g4",)])
        for dco in range(NDC):
            pm = ps["ps_m0"] if dco % 2 == 0 else ps["ps_m1"]
            pmk = ("ps_m", dco % 2)
            for ic in range(8):
                MM(P, pm[:, 0:4], WB[:, 16384 + ic * D + dco * 128:16384 + ic * D + (dco + 1) * 128],
                   sb["g4"][:, ic * 4:(ic + 1) * 4], ic == 0, ic == 7,
                   reads=[("WB", 16 + 2 * ic + dco // 8), ("g4",)], writes=[pmk])
            COPY(P, sb["p4"][:, dco * 4:(dco + 1) * 4], pm[:, 0:4], reads=[pmk], writes=[("p4",)])
        DMA(P, "pool", io["cc_in"][:, :], sb["p4"][:], reads=[("p4",)], writes=[("ccin",)])
        CCOMP(P, "AllGather", OP.bypass, [io["cc_in"].opt()], [io["cc_out"].opt()], groups,
              reads=[("ccin",)], writes=[("ccout",)])
        for r in range(8):
            pr_ = r % 2
            DMA(P, "syn", sb["agst"][:, pr_ * 64:(pr_ + 1) * 64], io["cc_out"][r * 128:(r + 1) * 128, :],
                reads=[("ccout",)], writes=[("agst", pr_)])
            if r == 0:
                COPY(P, sb["p4"][:], sb["agst"][:, 0:64], reads=[("agst", 0)], writes=[("p4",)])
            else:
                TT(P, sb["p4"][:], sb["p4"][:], sb["agst"][:, pr_ * 64:(pr_ + 1) * 64], OP.add,
                   reads=[("agst", pr_), ("p4",)], writes=[("p4",)])
        TT(P, sb["p4"][:], sb["p4"][:], sb["d2b4"][:], OP.add, reads=[("d2b4",), ("p4",)], writes=[("p4",)])
        TT(P, th4[:], th4[:], sb["p4"][:], OP.add, reads=[("p4",), ("th4",)], writes=[("th4",)])
    DMA(P, "pool", io["dbg_th"][:, :], th4[:], reads=[("th4",)], writes=[("dbg1",)])

    COPY(P, tbf4[:], th4[:], reads=[("th4",)], writes=[("tbf4",)])
    MEMSET(P, sb["bc4"][:], 0.0, writes=[("bc4",)])
    matvec4(io["bc_bf"], tbf4, "tbf4", sb["bc4"], "bc4")
    TT(P, sb["bc4"][:], sb["bc4"][:], sb["bcb4"][:], OP.add, reads=[("bcb4",), ("bc4",)], writes=[("bc4",)])
    DMA(P, "pool", io["dbg_bc"][:, :], sb["bc4"][:], reads=[("bc4",)], writes=[("dbg2",)])
    MM(P, ps["ps_x"][:, 0:4], sb["ones1r"][:], sb["ind"][0:1, :], True, True,
       reads=[("ones1r",), ("ind",)], writes=[("ps_x",)])
    ACTV(P, sb["indb"][:], ps["ps_x"][:, 0:4], AF.Copy, reads=[("ps_x",)], writes=[("indb",)])
    MEMSET(P, sb["bcfm"][:], 0.0, writes=[("bcfm",)])
    for b in range(4):
        TS(P, sb["vtmp"][:], bview(sb["bc4"], b, NDC), sb["indb"][:, b:b + 1], None, OP.mult,
           reads=[("bc4",), ("indb",)], writes=[("vtmp",)])
        TT(P, sb["bcfm"][:], sb["bcfm"][:], sb["vtmp"][:], OP.add,
           reads=[("vtmp",), ("bcfm",)], writes=[("bcfm",)])

    for dci in range(NDC):
        DMA(P, "syn", WB[:, dci * D:(dci + 1) * D], io["gate_w_bf"][dci * 128:(dci + 1) * 128, :],
            writes=[("WB", 2 * dci), ("WB", 2 * dci + 1)])

    for os_ in range(4):
        tsl = slice(os_ * SLOT, (os_ + 1) * SLOT)
        for dc in range(NDC):
            j = dc % 2
            xst = sb["xstg"][:, j * SLOT:(j + 1) * SLOT]
            DMA(P, "syn", xst, io["x1T"][dc * 128:(dc + 1) * 128, tsl], writes=[("xstg", j)])
            COPY(P, sb["X1"][:, dc * SLOT:(dc + 1) * SLOT], xst, reads=[("xstg", j)], writes=[("X1", dc)])
            COPY(P, sb["x1b"][:, dc * SLOT:(dc + 1) * SLOT], sb["X1"][:, dc * SLOT:(dc + 1) * SLOT],
                 reads=[("X1", dc)], writes=[("x1b", dc)])
        for dc2 in range(NDC):
            par = dc2 % 2
            pw = ps["ps_w0"] if par == 0 else ps["ps_w1"]
            pwk = ("ps_w", par)
            for dci in range(NDC):
                MM(P, pw[:], WB[:, dci * D + dc2 * 128:dci * D + (dc2 + 1) * 128],
                   sb["x1b"][:, dci * SLOT:(dci + 1) * SLOT], dci == 0, dci == NDC - 1,
                   reads=[("WB", 2 * dci + dc2 // 8), ("x1b", dci)], writes=[pwk])
            gt = sb["gtile"][:, par * SLOT:(par + 1) * SLOT]
            ACTV(P, gt, pw[:], AF.Sigmoid, bias=sb["gb"][:, dc2:dc2 + 1],
                 reads=[pwk, ("gb",)], writes=[("gtile", par)])
            dd = sb["Dt"][:, dc2 * SLOT:(dc2 + 1) * SLOT]
            STT(P, dd, gt, sb["bcfm"][:, dc2:dc2 + 1], sb["X1"][:, dc2 * SLOT:(dc2 + 1) * SLOT],
                OP.mult, OP.add, reads=[("gtile", par), ("bcfm",), ("X1", dc2)], writes=[("Dt", dc2)])
        ln_stats(P, sb, ps, lambda i: sb["Dt"][:, i * SLOT:(i + 1) * SLOT],
                 lambda i: ("Dt", i), NDC, 1e-6, "f0a")
        for dc in range(NDC):
            dd = sb["Dt"][:, dc * SLOT:(dc + 1) * SLOT]
            TT(P, dd, dd, sb["lna"][:], OP.mult, reads=[("lna",), ("Dt", dc)], writes=[("Dt", dc)])
            TT(P, dd, dd, sb["lnb"][:], OP.add, reads=[("lnb",), ("Dt", dc)], writes=[("Dt", dc)])
            TS(P, dd, dd, sb["og"][:, dc:dc + 1], sb["ob"][:, dc:dc + 1], OP.mult, OP.add,
               reads=[("og",), ("ob",), ("Dt", dc)], writes=[("Dt", dc)])
            TT(P, sb["X1"][:, dc * SLOT:(dc + 1) * SLOT], sb["X1"][:, dc * SLOT:(dc + 1) * SLOT], dd,
               OP.add, reads=[("Dt", dc), ("X1", dc)], writes=[("X1", dc)])
        ln_stats(P, sb, ps, lambda i: sb["X1"][:, i * SLOT:(i + 1) * SLOT],
                 lambda i: ("X1", i), NDC, 1e-5, "f0b")
        for dc in range(NDC):
            x1 = sb["X1"][:, dc * SLOT:(dc + 1) * SLOT]
            xo = sb["xot"][:, (dc % 2) * SLOT:(dc % 2 + 1) * SLOT]
            TT(P, xo, x1, sb["lna"][:], OP.mult, reads=[("lna",), ("X1", dc)], writes=[("xot", dc % 2)])
            TT(P, xo, xo, sb["lnb"][:], OP.add, reads=[("lnb",), ("xot", dc % 2)], writes=[("xot", dc % 2)])
            DMA(P, "pool", io["x2T"][dc * 128:(dc + 1) * 128, tsl], xo,
                reads=[("xot", dc % 2)], writes=[("x2d", os_, dc)])
            if "x2cc" in io:
                xob = sb["xobf"][:, (dc % 2) * SLOT:(dc % 2 + 1) * SLOT]
                COPY(P, xob, xo, reads=[("xot", dc % 2)], writes=[("xobf", dc % 2)])
                DMA(P, "pool", io["x2cc"][dc * 128:(dc + 1) * 128, tsl], xob,
                    reads=[("xobf", dc % 2)], writes=[("x2ccd", os_, dc)])
    FIN(P, u[0:1, 63:64])


def f0_sbuf(add):
    sb = {}
    sb["ones128"] = add("ones128", [128, 1], F32)
    sb["ones1r"] = add("ones1r", [1, 128], F32)
    sb["u"] = add("u", [1, 64], F32)
    sb["vb"] = add("vb", [128, 2], F32)
    sb["vsq"] = add("vsq", [128, NDC], F32)
    for n in ["ing4", "inb4", "png4", "pnb4", "aggb4", "bcb4", "d2b4", "xlast4"]:
        sb[n] = add(n, [128, 4 * NDC], F32)
    for n in ["gb", "og", "ob"]:
        sb[n] = add(n, [128, NDC], F32)
    sb["d1b4"] = add("d1b4", [128, 32], F32)
    sb["ind"] = add("ind", [1, 4], F32)
    sb["indb"] = add("indb", [128, 4], F32)
    sb["vtmp"] = add("vtmp", [128, NDC], F32)
    sb["tfm4"] = add("tfm4", [128, 4 * NDC], F32)
    sb["tbf4"] = add("tbf4", [128, 4 * NDC], BF16)
    sb["th4"] = add("th4", [128, 4 * NDC], F32)
    sb["h4"] = add("h4", [128, 32], F32)
    sb["g4"] = add("g4", [128, 32], BF16)
    sb["p4"] = add("p4", [128, 4 * NDC], F32)
    sb["agst"] = add("agst", [128, 128], F32)
    sb["bc4"] = add("bc4", [128, 4 * NDC], F32)
    sb["bcfm"] = add("bcfm", [128, NDC], F32)
    sb["WB"] = add("WB", [128, 32768], BF16)
    sb["wstg"] = add("wstg", [128, 2 * D], BF16)
    sb["xstg"] = add("xstg", [128, 2 * SLOT], F32)
    sb["X1"] = add("X1", [128, NDC * SLOT], F32)
    sb["x1b"] = add("x1b", [128, NDC * SLOT], BF16)
    sb["Dt"] = add("Dt", [128, NDC * SLOT], F32)
    sb["gtile"] = add("gtile", [128, 2 * SLOT], F32)
    sb["xot"] = add("xot", [128, 2 * SLOT], F32)
    sb["xobf"] = add("xobf", [128, 2 * SLOT], BF16)
    sb["sq"] = add("sq", [128, SLOT], F32)
    sb["lnr"] = add("lnr", [1, 8 * SLOT], F32)
    sb["lna"] = add("lna", [128, SLOT], F32)
    sb["lnb"] = add("lnb", [128, SLOT], F32)
    sb["epst"] = add("epst", [128, 1], F32)
    return sb


F0_IO = [
    ("x1T", [D, TC], F32, "ExternalInput"),
    ("xlast4", [128, 4 * NDC], F32, "ExternalInput"),
    ("ing4", [128, 4 * NDC], F32, "ExternalInput"),
    ("inb4", [128, 4 * NDC], F32, "ExternalInput"),
    ("png4", [128, 4 * NDC], F32, "ExternalInput"),
    ("pnb4", [128, 4 * NDC], F32, "ExternalInput"),
    ("aggb4", [128, 4 * NDC], F32, "ExternalInput"),
    ("bcb4", [128, 4 * NDC], F32, "ExternalInput"),
    ("gb", [128, NDC], F32, "ExternalInput"),
    ("og", [128, NDC], F32, "ExternalInput"),
    ("ob", [128, NDC], F32, "ExternalInput"),
    ("ind", [1, 4], F32, "ExternalInput"),
    ("d1b4", [128, 32], F32, "ExternalInput"),
    ("d2b4", [128, 4 * NDC], F32, "ExternalInput"),
    ("gate_w_bf", [D, D], BF16, "ExternalInput"),
    ("agg_bf", [D, D], BF16, "ExternalInput"),
    ("bc_bf", [D, D], BF16, "ExternalInput"),
    ("d1_bf", [D, 1024], BF16, "ExternalInput"),
    ("d2_bf", [1024, D], BF16, "ExternalInput"),
    ("x2T", [D, TC], F32, "ExternalOutput"),
    ("dbg_th", [128, 4 * NDC], F32, "ExternalOutput"),
    ("dbg_bc", [128, 4 * NDC], F32, "ExternalOutput"),
    ("dbg_t0", [128, 4 * NDC], F32, "ExternalOutput"),
]
F0_INT = [("cc_in", [128, 4 * NDC], F32), ("cc_out", [1024, 4 * NDC], F32)]


def build_f0():
    return build_phase(emit_f0, F0_IO, f0_sbuf, F0_INT)



def emit_l1a(P, nc, io, sb, ps):
    qT_hbm = io["qT_hbm"]; part1 = io["part1"]
    xsrc = io.get("xsrc")
    if xsrc is None:
        xT = io["xT"]
        xsrc = lambda s_: (xT, s_ * SLOT)
    x_bf16 = io.get("x_bf16", False)
    DMA(P, "syn", sb["mdiag"][:], io["mdiag"], writes=[("mdiag",)])
    MEMSET(P, sb["ones128"][:], 1.0, writes=[("ones128",)])
    MEMSET(P, sb["ones1r"][:], 1.0, writes=[("ones1r",)])
    MEMSET(P, sb["V"][:], 1.0, writes=[("Vinit",)])

    wst = sb["wstage"]
    cast_resident(P, io["wq_sl"], sb["bigw"], wst, "bigw", NDC, 1024)
    cast_resident(P, io["wk_sl"], sb["wk_bf"], wst, "wk", NDC, 256)
    cast_resident(P, io["wv_sl"], sb["wv_bf"], wst, "wv", NDC, 256)

    for s_ in range(NSLOT):
        tsl = slice(s_ * SLOT, (s_ + 1) * SLOT)
        xt_, xc0 = xsrc(s_)
        xcsl = slice(xc0, xc0 + SLOT)
        for dc in range(NDC):
            if x_bf16:
                DMA(P, "syn", sb["xb"][:, dc * SLOT:(dc + 1) * SLOT],
                    xt_[dc * 128:(dc + 1) * 128, xcsl], writes=[("xb", dc)])
            else:
                j = dc % 4
                stg = sb["wstage"][:, j * SLOT:(j + 1) * SLOT]
                DMA(P, "syn", stg, xt_[dc * 128:(dc + 1) * 128, xcsl], writes=[("stg", j)])
                COPY(P, sb["xb"][:, dc * SLOT:(dc + 1) * SLOT], stg,
                     reads=[("stg", j)], writes=[("xb", dc)])
        for i, tab in enumerate(["qcos", "qsin", "kcos", "ksin"]):
            DMA(P, "syn", sb["tabs"][:, i * SLOT:(i + 1) * SLOT], io[tab][:, tsl], writes=[("tab", i)])

        for hp in range(8):
            par = hp % 2
            pw = ps["ps_w0"] if par == 0 else ps["ps_w1"]
            pwk = ("ps_w", par)
            for dc in range(NDC):
                MM(P, pw[:], sb["bigw"][:, dc * 1024 + hp * 128:dc * 1024 + (hp + 1) * 128],
                   sb["xb"][:, dc * SLOT:(dc + 1) * SLOT], dc == 0, dc == NDC - 1,
                   reads=[("bigw", dc), ("xb", dc)], writes=[pwk])
            qo = sb["qout"][:, par * SLOT:(par + 1) * SLOT]
            qc, qs_ = sb["tabs"][:, 0:SLOT], sb["tabs"][:, SLOT:2 * SLOT]
            for half in range(2):
                b0 = half * 64
                rt = sb["rtmp"]
                TT(P, rt[0:32, 0:SLOT], pw[b0:b0 + 32, :], qc[0:32, :], OP.mult,
                   reads=[pwk, ("tab", 0)], writes=[("rt", 0)])
                TT(P, rt[0:32, SLOT:2 * SLOT], pw[b0 + 32:b0 + 64, :], qs_[0:32, :], OP.mult,
                   reads=[pwk, ("tab", 1)], writes=[("rt", 1)])
                TT(P, qo[b0:b0 + 32, :], rt[0:32, 0:SLOT], rt[0:32, SLOT:2 * SLOT], OP.subtract,
                   reads=[("rt", 0), ("rt", 1)], writes=[("qout", par, 0)])
                TT(P, rt[0:32, 0:SLOT], pw[b0:b0 + 32, :], qs_[0:32, :], OP.mult,
                   reads=[pwk, ("tab", 1)], writes=[("rt", 0)])
                TT(P, rt[0:32, SLOT:2 * SLOT], pw[b0 + 32:b0 + 64, :], qc[0:32, :], OP.mult,
                   reads=[pwk, ("tab", 0)], writes=[("rt", 1)])
                TT(P, qo[b0 + 32:b0 + 64, :], rt[0:32, 0:SLOT], rt[0:32, SLOT:2 * SLOT], OP.add,
                   reads=[("rt", 0), ("rt", 1)], writes=[("qout", par, 1)])
            DMA(P, "syn", qT_hbm[hp * 128:(hp + 1) * 128, tsl], qo,
                reads=[("qout", par, 0), ("qout", par, 1)], writes=[("qhbm", hp, s_)])

        kc_t, ks_t = sb["tabs"][:, 2 * SLOT:3 * SLOT], sb["tabs"][:, 3 * SLOT:4 * SLOT]
        for kp in range(2):
            pk = ps["ps_m0"] if kp == 0 else ps["ps_m1"]
            pkk = ("ps_m", kp)
            for dc in range(NDC):
                MM(P, pk[:], sb["wk_bf"][:, dc * 256 + kp * 128:dc * 256 + (kp + 1) * 128],
                   sb["xb"][:, dc * SLOT:(dc + 1) * SLOT], dc == 0, dc == NDC - 1,
                   reads=[("wk", dc), ("xb", dc)], writes=[pkk])
            for half in range(2):
                b0 = half * 64
                rt = sb["rtmp"]
                kd0 = sb["K1"][b0:b0 + 32, kp * T + s_ * SLOT:kp * T + (s_ + 1) * SLOT]
                kd1 = sb["K1"][b0 + 32:b0 + 64, kp * T + s_ * SLOT:kp * T + (s_ + 1) * SLOT]
                TT(P, rt[0:32, 0:SLOT], pk[b0:b0 + 32, :], kc_t[0:32, :], OP.mult,
                   reads=[pkk, ("tab", 2)], writes=[("rt", 0)])
                TT(P, rt[0:32, SLOT:2 * SLOT], pk[b0 + 32:b0 + 64, :], ks_t[0:32, :], OP.mult,
                   reads=[pkk, ("tab", 3)], writes=[("rt", 1)])
                TT(P, kd0, rt[0:32, 0:SLOT], rt[0:32, SLOT:2 * SLOT], OP.subtract,
                   reads=[("rt", 0), ("rt", 1)], writes=[("K1", kp, half, s_)])
                TT(P, rt[0:32, 0:SLOT], pk[b0:b0 + 32, :], ks_t[0:32, :], OP.mult,
                   reads=[pkk, ("tab", 3)], writes=[("rt", 0)])
                TT(P, rt[0:32, SLOT:2 * SLOT], pk[b0 + 32:b0 + 64, :], kc_t[0:32, :], OP.mult,
                   reads=[pkk, ("tab", 2)], writes=[("rt", 1)])
                TT(P, kd1, rt[0:32, 0:SLOT], rt[0:32, SLOT:2 * SLOT], OP.add,
                   reads=[("rt", 0), ("rt", 1)], writes=[("K1", kp, half, s_, 1)])

        for tcl in range(4):
            kcc = 4 * s_ + tcl
            par = tcl % 2
            pv_ = ps["ps_w0"] if par == 0 else ps["ps_w1"]
            pvk = ("ps_w", par)
            for dc in range(NDC):
                MM(P, pv_[:, 0:256], sb["xb"][:, dc * SLOT + tcl * 128:dc * SLOT + (tcl + 1) * 128],
                   sb["wv_bf"][:, dc * 256:(dc + 1) * 256], dc == 0, dc == NDC - 1,
                   reads=[("wv", dc), ("xb", dc)], writes=[pvk])
            for kv in range(4):
                COPY(P, sb["V"][:, kcc * 260 + kv * 65:kcc * 260 + kv * 65 + 64],
                     pv_[:, kv * 64:(kv + 1) * 64],
                     reads=[pvk, ("Vinit",)], writes=[("V", kcc, kv)])

    u = sb["u"]
    for kv in range(4):
        kp, half = kv // 2, kv % 2
        b0 = half * 64
        for cc in range(8):
            ACTV(P, sb["sq"][0:64, :], sb["K1"][b0:b0 + 64, kp * T + cc * SLOT:kp * T + (cc + 1) * SLOT],
                 AF.Square, reads=[("K1", kp, half, cc), ("K1", kp, half, cc, 1)], writes=[("sq",)])
            MM(P, ps["ps_x"][0:1, :], sb["ones128"][0:64, :], sb["sq"][0:64, :], True, True,
               reads=[("ones128",), ("sq",)], writes=[("ps_x",)])
            REDMAX(P, sb["k2"][0:1, cc:cc + 1], ps["ps_x"][0:1, 0:SLOT],
                   reads=[("ps_x",)], writes=[("k2", cc)])
        REDMAX(P, u[0:1, 40 + kv:41 + kv], sb["k2"][0:1, 0:8],
               reads=[("k2", cc2) for cc2 in range(8)], writes=[("u",)])
        ACTV(P, u[0:1, 44 + kv:45 + kv], u[0:1, 40 + kv:41 + kv], AF.Ln,
             reads=[("u",)], writes=[("umk", kv)])

    for hc in range(8):
        for half in range(2):
            i = 2 * hc + half
            p = i % 2
            st = wst[:, p * 1024:(p + 1) * 1024]
            DMA(P, "syn", st, io["wo_sl"][hc * 128:(hc + 1) * 128, half * 1024:(half + 1) * 1024],
                writes=stg_keys(p, 1024))
            COPY(P, sb["bigw"][:, i * 1024:(i + 1) * 1024], st,
                 reads=stg_keys(p, 1024), writes=[("bigw", i)])

    for s_ in range(NSLOT):
        tsl = slice(s_ * SLOT, (s_ + 1) * SLOT)
        nkc = 4 * s_ + 4
        for h in range(16):
            kv = h // 4
            base = (kv % 2) * 64
            DMA(P, "syn", sb["Qatt"][base:base + 64, h * SLOT:(h + 1) * SLOT],
                qT_hbm[(h // 2) * 128 + (h % 2) * 64:(h // 2) * 128 + (h % 2) * 64 + 64, tsl],
                reads=[("qhbm", h // 2, s_)], writes=[("Qatt", h)])
        for h in range(16):
            kv = h // 4
            base = (kv % 2) * 64
            qt = sb["Qatt"][base:base + 64, h * SLOT:(h + 1) * SLOT]
            ACTV(P, sb["sq"][0:64, :], qt, AF.Square, reads=[("Qatt", h)], writes=[("sq",)])
            MM(P, ps["ps_x"][0:1, :], sb["ones128"][0:64, :], sb["sq"][0:64, :], True, True,
               reads=[("ones128",), ("sq",)], writes=[("ps_x",)])
            REDMAX(P, u[0:1, 50:51], ps["ps_x"][0:1, 0:SLOT], reads=[("ps_x",)], writes=[("u",)])
            ACTV(P, u[0:1, 51:52], u[0:1, 50:51], AF.Ln, reads=[("u",)], writes=[("u",)])
            TT(P, u[0:1, 52:53], u[0:1, 51:52], u[0:1, 44 + kv:45 + kv], OP.add,
               reads=[("u",), ("umk", kv)], writes=[("u",)])
            ACTV(P, u[0:1, 52:53], u[0:1, 52:53], AF.Exp, scale=0.5, reads=[("u",)], writes=[("u",)])
            TS(P, u[0:1, 52:53], u[0:1, 52:53], -1.0, None, OP.mult, reads=[("u",)], writes=[("u",)])
            MM(P, ps["ps_x"][:, 24:25], sb["ones1r"][:], u[0:1, 52:53], True, True,
               reads=[("ones1r",), ("u",)], writes=[("ps_x",)])
            ACTV(P, sb["nub"][:], ps["ps_x"][:, 24:25], AF.Copy, reads=[("ps_x",)], writes=[("nub",)])
            QKPS = ["ps_qk0", "ps_qk1", "ps_m0", "ps_m1"]
            QKK = [("ps_qk", 0), ("ps_qk", 1), ("ps_m", 0), ("ps_m", 1)]

            def front(kcc):
                par = kcc % 4
                pq = ps[QKPS[par]]
                pqk = QKK[par]
                kp, half = kv // 2, kv % 2
                b0 = half * 64
                MM(P, pq[:], sb["K1"][b0:b0 + 64, kp * T + kcc * 128:kp * T + (kcc + 1) * 128],
                   qt, True, True,
                   reads=[("K1", kp, half, kcc // 4), ("K1", kp, half, kcc // 4, 1), ("Qatt", h)],
                   writes=[pqk])
                if kcc >= 4 * s_:
                    rr = kcc - 4 * s_
                    tdst = sb["tsc"][:, (kcc % 2) * SLOT:(kcc % 2 + 1) * SLOT]
                    TT(P, tdst, pq[:], sb["mdiag"][:, rr * SLOT:(rr + 1) * SLOT], OP.add,
                       reads=[("mdiag",), pqk], writes=[("tsc", kcc % 2)])
                    src, srck = tdst, ("tsc", kcc % 2)
                else:
                    src, srck = pq[:], pqk
                ACTV(P, sb["Pp"][:, (kcc % 2) * SLOT:(kcc % 2 + 1) * SLOT], src, AF.Exp,
                     bias=sb["nub"][:, 0:1],
                     reads=[srck, ("nub",)], writes=[("Pp", kcc % 2)])

            def back(kcc):
                MM(P, ps["ps_pv"][0:65, :], sb["V"][:, kcc * 260 + kv * 65:kcc * 260 + (kv + 1) * 65],
                   sb["Pp"][:, (kcc % 2) * SLOT:(kcc % 2 + 1) * SLOT], kcc == 0, kcc == nkc - 1,
                   reads=[("Pp", kcc % 2), ("V", kcc, kv)], writes=[("ps_pv",)])

            for kcc in range(nkc):
                if kcc >= 2:
                    back(kcc - 2)
                front(kcc)
            back(nkc - 2)
            back(nkc - 1)
            RECIP(P, sb["rl"][0:1, :], ps["ps_pv"][64:65, :], reads=[("ps_pv",)], writes=[("rl",)])
            MM(P, ps["ps_x"][:, 0:SLOT], sb["ones1r"][:], sb["rl"][0:1, :], True, True,
               reads=[("ones1r",), ("rl",)], writes=[("ps_x",)])
            ACTV(P, sb["rb"][:], ps["ps_x"][:, 0:SLOT], AF.Copy, reads=[("ps_x",)], writes=[("rb",)])
            TT(P, sb["AO"][(h % 2) * 64:(h % 2) * 64 + 64, (h // 2) * SLOT:(h // 2 + 1) * SLOT],
               ps["ps_pv"][0:64, :], sb["rb"][0:64, :], OP.mult,
               reads=[("ps_pv",), ("rb",)], writes=[("AO", h)])
        for dc2 in range(NDC):
            par = dc2 % 2
            pw = ps["ps_w0"] if par == 0 else ps["ps_w1"]
            pwk = ("ps_w", par)
            for hp in range(8):
                MM(P, pw[:], sb["bigw"][:, hp * D + dc2 * 128:hp * D + (dc2 + 1) * 128],
                   sb["AO"][:, hp * SLOT:(hp + 1) * SLOT], hp == 0, hp == 7,
                   reads=[("AO", 2 * hp), ("AO", 2 * hp + 1), ("bigw", 2 * hp + dc2 // 8)],
                   writes=[pwk])
            ACTV(P, sb["po"][:, par * SLOT:(par + 1) * SLOT], pw[:], AF.Copy,
                 reads=[pwk], writes=[("po", par)])
            DMA(P, "pool", part1[dc2 * 128:(dc2 + 1) * 128, tsl], sb["po"][:, par * SLOT:(par + 1) * SLOT],
                reads=[("po", par)], writes=[("part", s_, dc2)])
    FIN(P, u[0:1, 63:64])


def l1a_sbuf(add):
    sb = {}
    sb["K1"] = add("K1", [128, 2 * T], BF16)
    sb["V"] = add("V", [128, NKC * 260], BF16)
    sb["mdiag"] = add("mdiag", [128, 4 * SLOT], BF16)
    sb["ones128"] = add("ones128", [128, 1], F32)
    sb["ones1r"] = add("ones1r", [1, 128], F32)
    sb["u"] = add("u", [1, 64], F32)
    sb["wstage"] = add("wstage", [128, 2048], F32)
    sb["bigw"] = add("bigw", [128, 16384], BF16)
    sb["wk_bf"] = add("wk_bf", [128, 4096], BF16)
    sb["wv_bf"] = add("wv_bf", [128, 4096], BF16)
    sb["xb"] = add("xb", [128, NDC * SLOT], BF16)
    sb["tabs"] = add("tabs", [32, 4 * SLOT], BF16)
    sb["rtmp"] = add("rtmp", [32, 2 * SLOT], F32)
    sb["qout"] = add("qout", [128, 2 * SLOT], BF16)
    sb["Qatt"] = add("Qatt", [128, 16 * SLOT], BF16)
    sb["Pp"] = add("Pp", [128, 2 * SLOT], BF16)
    sb["tsc"] = add("tsc", [128, 2 * SLOT], F32)
    sb["AO"] = add("AO", [128, 8 * SLOT], BF16)
    sb["sq"] = add("sq", [64, SLOT], F32)
    sb["rl"] = add("rl", [1, SLOT], F32)
    sb["rb"] = add("rb", [128, SLOT], F32)
    sb["nub"] = add("nub", [128, 1], F32)
    sb["po"] = add("po", [128, 2 * SLOT], BF16)
    sb["k2"] = add("k2", [1, 64], F32)
    return sb


L1A_IO = [
    ("xT", [D, T], F32, "ExternalInput"),
    ("wq_sl", [D, 1024], F32, "ExternalInput"),
    ("wk_sl", [D, 256], F32, "ExternalInput"),
    ("wv_sl", [D, 256], F32, "ExternalInput"),
    ("wo_sl", [1024, D], F32, "ExternalInput"),
    ("qcos", [32, T], BF16, "ExternalInput"),
    ("qsin", [32, T], BF16, "ExternalInput"),
    ("kcos", [32, T], BF16, "ExternalInput"),
    ("ksin", [32, T], BF16, "ExternalInput"),
    ("mdiag", [128, 4 * SLOT], BF16, "ExternalInput"),
    ("part1", [D, T], BF16, "ExternalOutput"),
]
L1A_INT = [("qT_hbm", [1024, T], BF16)]


def build_l1a():
    return build_phase(emit_l1a, L1A_IO, l1a_sbuf, L1A_INT)


def l1a_in_maps(inputs, X2):
    Wq = np.asarray(inputs["L1_attn_Wq"], np.float32)
    Wk = np.asarray(inputs["L1_attn_Wk"], np.float32)
    Wv = np.asarray(inputs["L1_attn_Wv"], np.float32)
    Wo = np.asarray(inputs["L1_attn_Wo"], np.float32)
    qc, qs = rope_tables(64, 32, 1.0 / math.sqrt(64.0))
    kc, ks = rope_tables(64, 32, 1.0)
    md = host_mdiag()
    maps = []
    for c in range(8):
        b, H = c // 2, c % 2
        maps.append({
            "xT": np.ascontiguousarray(np.asarray(X2[b], np.float32).T),
            "wq_sl": np.ascontiguousarray(Wq[:, H * 1024:(H + 1) * 1024]),
            "wk_sl": np.ascontiguousarray(Wk[:, H * 256:(H + 1) * 256]),
            "wv_sl": np.ascontiguousarray(Wv[:, H * 256:(H + 1) * 256]),
            "wo_sl": np.ascontiguousarray(Wo[H * 1024:(H + 1) * 1024, :]),
            "qcos": qc, "qsin": qs, "kcos": kc, "ksin": ks, "mdiag": md,
        })
    return maps



F1A_BW = {
    "kvd": (0, 8), "qd": (8, 8), "ku": (16, 2), "qu": (18, 2),
    "vu": (20, 4), "ropek": (24, 8), "ropeq": (32, 2),
}


def bw_keys(name, u0=None, u1=None):
    base, n = F1A_BW[name]
    a = base if u0 is None else base + u0
    b = base + n if u1 is None else base + u1
    return [("bigw", i) for i in range(a, b)]


def emit_f1a(P, nc, io, sb, ps):
    part2 = io["part2"]
    xsrc = io.get("xsrc")
    if xsrc is None:
        xmT = io["xmT"]
        xsrc = lambda s_: (xmT, s_ * SLOT)
    k_hbm, v_hbm, q_hbm = io["k_hbm"], io["v_hbm"], io["q_hbm"]
    DMA(P, "syn", sb["mdiag"][:], io["mdiag"], writes=[("mdiag",)])
    MEMSET(P, sb["ones128"][:], 1.0, writes=[("ones128",)])
    MEMSET(P, sb["ones1r"][:], 1.0, writes=[("ones1r",)])
    bw = sb["bigw"]
    wst = sb["wstage"]
    S128 = 1.0 / math.sqrt(128.0)

    for dc in range(NDC):
        DMA(P, "syn", bw[:, dc * SLOT:(dc + 1) * SLOT], io["kvd_bf"][dc * 128:(dc + 1) * 128, :],
            writes=[("bigw", dc // 2)] if dc % 2 == 1 else [("bigw", dc // 2)])
        DMA(P, "syn", bw[:, 8192 + dc * SLOT:8192 + (dc + 1) * SLOT], io["qd_bf"][dc * 128:(dc + 1) * 128, :],
            writes=[("bigw", 8 + dc // 2)])
    for (name, src, cols, nch) in [("ku", "k_u_sl", 512, 4), ("qu", "q_u_sl", 512, 4),
                                   ("vu", "v_u_sl", 1024, 4), ("ropek", "rope_k_sl", 512, 16),
                                   ("ropeq", "rope_q_sl", 512, 4)]:
        base, _n = F1A_BW[name]
        for dc in range(nch):
            p = dc % 2
            st = wst[:, p * 1024:p * 1024 + cols]
            DMA(P, "syn", st, io[src][dc * 128:(dc + 1) * 128, :], writes=stg_keys(p, cols))
            COPY(P, bw[:, base * 1024 + dc * cols:base * 1024 + (dc + 1) * cols], st,
                 reads=stg_keys(p, cols), writes=[("bigw", base + (dc * cols) // 1024)])

    for s_ in range(NSLOT):
        tsl = slice(s_ * SLOT, (s_ + 1) * SLOT)
        xt_, xc0 = xsrc(s_)
        xcsl = slice(xc0, xc0 + SLOT)
        for dc in range(NDC):
            DMA(P, "syn", sb["xb"][:, dc * SLOT:(dc + 1) * SLOT], xt_[dc * 128:(dc + 1) * 128, xcsl],
                writes=[("xb", dc)])
        for i, tab in enumerate(["qcos", "qsin", "kcos", "ksin"]):
            DMA(P, "syn", sb["tabs"][:, i * SLOT:(i + 1) * SLOT], io[tab][:, tsl], writes=[("tab", i)])

        for which, bwn, dstk in (("kvd", "kvd", "kvds"), ("qd", "qd", "qds")):
            base, _ = F1A_BW[bwn]
            for latc in range(4):
                pm = ps["ps_m0"] if latc % 2 == 0 else ps["ps_m1"]
                pmk = ("ps_m", latc % 2)
                for dc in range(NDC):
                    MM(P, pm[:], bw[:, base * 1024 + dc * SLOT + latc * 128:base * 1024 + dc * SLOT + (latc + 1) * 128],
                       sb["xb"][:, dc * SLOT:(dc + 1) * SLOT], dc == 0, dc == NDC - 1,
                       reads=[("bigw", base + dc // 2), ("xb", dc)], writes=[pmk])
                COPY(P, sb[dstk][:, latc * SLOT:(latc + 1) * SLOT], pm[:],
                     reads=[pmk], writes=[(dstk, latc)])

        kc_t, ks_t = sb["tabs"][:, 2 * SLOT:3 * SLOT], sb["tabs"][:, 3 * SLOT:4 * SLOT]
        for hp in range(4):
            pkp = ps["ps_w0"]
            pkr = ps["ps_w1"]
            bku, _ = F1A_BW["ku"]
            for latc in range(4):
                MM(P, pkp[:], bw[:, bku * 1024 + latc * SLOT + hp * 128:bku * 1024 + latc * SLOT + (hp + 1) * 128],
                   sb["kvds"][:, latc * SLOT:(latc + 1) * SLOT], latc == 0, latc == 3,
                   reads=[("bigw", bku + latc // 2), ("kvds", latc)], writes=[("ps_w", 0)])
            brk, _ = F1A_BW["ropek"]
            for dc in range(NDC):
                MM(P, pkr[:], bw[:, brk * 1024 + dc * SLOT + hp * 128:brk * 1024 + dc * SLOT + (hp + 1) * 128],
                   sb["xb"][:, dc * SLOT:(dc + 1) * SLOT], dc == 0, dc == NDC - 1,
                   reads=[("bigw", brk + dc // 2), ("xb", dc)], writes=[("ps_w", 1)])
            for half in range(2):
                h = 2 * hp + half
                b0 = half * 64
                ko = sb["kout"][:, half * SLOT:(half + 1) * SLOT]
                COPY(P, ko[0:64, :], pkp[b0:b0 + 64, :], reads=[("ps_w", 0)], writes=[("kout", half, 0)])
                rt = sb["rtmp"]
                TT(P, rt[0:32, 0:SLOT], pkr[b0:b0 + 32, :], kc_t[0:32, :], OP.mult,
                   reads=[("ps_w", 1), ("tab", 2)], writes=[("rt", 0)])
                TT(P, rt[0:32, SLOT:2 * SLOT], pkr[b0 + 32:b0 + 64, :], ks_t[0:32, :], OP.mult,
                   reads=[("ps_w", 1), ("tab", 3)], writes=[("rt", 1)])
                TT(P, ko[64:96, :], rt[0:32, 0:SLOT], rt[0:32, SLOT:2 * SLOT], OP.subtract,
                   reads=[("rt", 0), ("rt", 1)], writes=[("kout", half, 1)])
                TT(P, rt[0:32, 0:SLOT], pkr[b0:b0 + 32, :], ks_t[0:32, :], OP.mult,
                   reads=[("ps_w", 1), ("tab", 3)], writes=[("rt", 0)])
                TT(P, rt[0:32, SLOT:2 * SLOT], pkr[b0 + 32:b0 + 64, :], kc_t[0:32, :], OP.mult,
                   reads=[("ps_w", 1), ("tab", 2)], writes=[("rt", 1)])
                TT(P, ko[96:128, :], rt[0:32, 0:SLOT], rt[0:32, SLOT:2 * SLOT], OP.add,
                   reads=[("rt", 0), ("rt", 1)], writes=[("kout", half, 2)])
                DMA(P, "syn", k_hbm[h * 128:(h + 1) * 128, tsl], ko,
                    reads=[("kout", half, 0), ("kout", half, 1), ("kout", half, 2)],
                    writes=[("khbm", h, s_)])

        for hp in range(4):
            pqp = ps["ps_w0"]
            pqr = ps["ps_w1"]
            bqu, _ = F1A_BW["qu"]
            for latc in range(4):
                MM(P, pqp[:], bw[:, bqu * 1024 + latc * SLOT + hp * 128:bqu * 1024 + latc * SLOT + (hp + 1) * 128],
                   sb["qds"][:, latc * SLOT:(latc + 1) * SLOT], latc == 0, latc == 3,
                   reads=[("bigw", bqu + latc // 2), ("qds", latc)], writes=[("ps_w", 0)])
            brq, _ = F1A_BW["ropeq"]
            for latc in range(4):
                MM(P, pqr[:], bw[:, brq * 1024 + latc * SLOT + hp * 128:brq * 1024 + latc * SLOT + (hp + 1) * 128],
                   sb["qds"][:, latc * SLOT:(latc + 1) * SLOT], latc == 0, latc == 3,
                   reads=[("bigw", brq + latc // 2), ("qds", latc)], writes=[("ps_w", 1)])
            for half in range(2):
                h = 2 * hp + half
                b0 = half * 64
                qo = sb["kout"][:, half * SLOT:(half + 1) * SLOT]
                P.step("act", lambda q, qo=qo, pqp=pqp, b0=b0: q.activation(
                    out=qo[0:64, :], in_=pqp[b0:b0 + 64, :], func=AF.Copy, scale=S128),
                    reads=[("ps_w", 0)], writes=[("kout", half, 0)], inc=1)
                rt = sb["rtmp"]
                qc, qs_ = sb["tabs"][:, 0:SLOT], sb["tabs"][:, SLOT:2 * SLOT]
                TT(P, rt[0:32, 0:SLOT], pqr[b0:b0 + 32, :], qc[0:32, :], OP.mult,
                   reads=[("ps_w", 1), ("tab", 0)], writes=[("rt", 0)])
                TT(P, rt[0:32, SLOT:2 * SLOT], pqr[b0 + 32:b0 + 64, :], qs_[0:32, :], OP.mult,
                   reads=[("ps_w", 1), ("tab", 1)], writes=[("rt", 1)])
                TT(P, qo[64:96, :], rt[0:32, 0:SLOT], rt[0:32, SLOT:2 * SLOT], OP.subtract,
                   reads=[("rt", 0), ("rt", 1)], writes=[("kout", half, 1)])
                TT(P, rt[0:32, 0:SLOT], pqr[b0:b0 + 32, :], qs_[0:32, :], OP.mult,
                   reads=[("ps_w", 1), ("tab", 1)], writes=[("rt", 0)])
                TT(P, rt[0:32, SLOT:2 * SLOT], pqr[b0 + 32:b0 + 64, :], qc[0:32, :], OP.mult,
                   reads=[("ps_w", 1), ("tab", 0)], writes=[("rt", 1)])
                TT(P, qo[96:128, :], rt[0:32, 0:SLOT], rt[0:32, SLOT:2 * SLOT], OP.add,
                   reads=[("rt", 0), ("rt", 1)], writes=[("kout", half, 2)])
                DMA(P, "syn", q_hbm[h * 128:(h + 1) * 128, tsl], qo,
                    reads=[("kout", half, 0), ("kout", half, 1), ("kout", half, 2)],
                    writes=[("qhbm", h, s_)])

        bvu, _ = F1A_BW["vu"]
        for tcl in range(4):
            for dvg in range(2):
                pv_ = ps["ps_w0"] if dvg == 0 else ps["ps_w1"]
                pvk = ("ps_w", dvg)
                for latc in range(4):
                    MM(P, pv_[:], sb["kvds"][:, latc * SLOT + tcl * 128:latc * SLOT + (tcl + 1) * 128],
                       bw[:, bvu * 1024 + latc * 1024 + dvg * 512:bvu * 1024 + latc * 1024 + (dvg + 1) * 512],
                       latc == 0, latc == 3,
                       reads=[("bigw", bvu + latc), ("kvds", latc)], writes=[pvk])
                COPY(P, sb["vout"][:, dvg * SLOT:(dvg + 1) * SLOT], pv_[:],
                     reads=[pvk], writes=[("vout", dvg)])
            DMA(P, "syn", v_hbm[(s_ * 4 + tcl) * 128:(s_ * 4 + tcl + 1) * 128, :], sb["vout"][:],
                reads=[("vout", 0), ("vout", 1)], writes=[("vhbm", s_, tcl)])

    for hc in range(8):
        for half in range(2):
            i = 2 * hc + half
            p = i % 2
            st = wst[:, p * 1024:(p + 1) * 1024]
            DMA(P, "syn", st, io["o_w_sl"][hc * 128:(hc + 1) * 128, half * 1024:(half + 1) * 1024],
                writes=stg_keys(p, 1024))
            COPY(P, bw[:, i * 1024:(i + 1) * 1024], st, reads=stg_keys(p, 1024), writes=[("bigw", i)])

    u = sb["u"]
    for s_ in range(NSLOT):
        tsl = slice(s_ * SLOT, (s_ + 1) * SLOT)
        nkc = 4 * s_ + 4
        for h in range(8):
            DMA(P, "syn", sb["Katt"][:, 0:nkc * 128], k_hbm[h * 128:(h + 1) * 128, 0:nkc * 128],
                reads=[("khbm", h, ss) for ss in range(s_ + 1)], writes=[("Katt",)])
            for kcc in range(nkc):
                DMA(P, "syn", sb["Vatt"][:, kcc * 128:(kcc + 1) * 128],
                    v_hbm[kcc * 128:(kcc + 1) * 128, h * 128:(h + 1) * 128],
                    reads=[("vhbm", kcc // 4, kcc % 4)], writes=[("Vatt", kcc)])
            DMA(P, "syn", sb["Qatt"][:, 0:SLOT], q_hbm[h * 128:(h + 1) * 128, tsl],
                reads=[("qhbm", h, s_)], writes=[("Qatt",)])
            for cc in range((nkc + 3) // 4):
                w_ = min(512, nkc * 128 - cc * 512)
                ACTV(P, sb["sq"][:, 0:w_], sb["Katt"][:, cc * 512:cc * 512 + w_], AF.Square,
                     reads=[("Katt",)], writes=[("sq",)])
                MM(P, ps["ps_x"][0:1, 0:w_], sb["ones128"][:], sb["sq"][:, 0:w_], True, True,
                   reads=[("ones128",), ("sq",)], writes=[("ps_x",)])
                REDMAX(P, sb["k2"][0:1, cc:cc + 1], ps["ps_x"][0:1, 0:w_],
                       reads=[("ps_x",)], writes=[("k2", cc)])
            REDMAX(P, u[0:1, 40:41], sb["k2"][0:1, 0:(nkc + 3) // 4],
                   reads=[("k2", cc2) for cc2 in range((nkc + 3) // 4)], writes=[("u",)])
            ACTV(P, u[0:1, 41:42], u[0:1, 40:41], AF.Ln, reads=[("u",)], writes=[("u",)])
            ACTV(P, sb["sq"][:], sb["Qatt"][:, 0:SLOT], AF.Square, reads=[("Qatt",)], writes=[("sq",)])
            MM(P, ps["ps_x"][0:1, :], sb["ones128"][:], sb["sq"][:], True, True,
               reads=[("ones128",), ("sq",)], writes=[("ps_x",)])
            REDMAX(P, u[0:1, 50:51], ps["ps_x"][0:1, 0:SLOT], reads=[("ps_x",)], writes=[("u",)])
            ACTV(P, u[0:1, 51:52], u[0:1, 50:51], AF.Ln, reads=[("u",)], writes=[("u",)])
            TT(P, u[0:1, 52:53], u[0:1, 51:52], u[0:1, 41:42], OP.add, reads=[("u",)], writes=[("u",)])
            ACTV(P, u[0:1, 52:53], u[0:1, 52:53], AF.Exp, scale=0.5, reads=[("u",)], writes=[("u",)])
            TS(P, u[0:1, 52:53], u[0:1, 52:53], -1.0, None, OP.mult, reads=[("u",)], writes=[("u",)])
            MM(P, ps["ps_x"][:, 24:25], sb["ones1r"][:], u[0:1, 52:53], True, True,
               reads=[("ones1r",), ("u",)], writes=[("ps_x",)])
            ACTV(P, sb["nub"][:], ps["ps_x"][:, 24:25], AF.Copy, reads=[("ps_x",)], writes=[("nub",)])
            QKPS = ["ps_qk0", "ps_qk1", "ps_m0", "ps_m1"]
            QKK = [("ps_qk", 0), ("ps_qk", 1), ("ps_m", 0), ("ps_m", 1)]

            def front(kcc):
                par = kcc % 4
                pq = ps[QKPS[par]]
                pqk = QKK[par]
                MM(P, pq[:], sb["Katt"][:, kcc * 128:(kcc + 1) * 128], sb["Qatt"][:, 0:SLOT],
                   True, True, reads=[("Katt",), ("Qatt",)], writes=[pqk])
                if kcc >= 4 * s_:
                    rr = kcc - 4 * s_
                    tdst = sb["tsc"][:, (kcc % 2) * SLOT:(kcc % 2 + 1) * SLOT]
                    TT(P, tdst, pq[:], sb["mdiag"][:, rr * SLOT:(rr + 1) * SLOT], OP.add,
                       reads=[("mdiag",), pqk], writes=[("tsc", kcc % 2)])
                    src, srck = tdst, ("tsc", kcc % 2)
                else:
                    src, srck = pq[:], pqk
                ACTV(P, sb["Pp"][:, (kcc % 2) * SLOT:(kcc % 2 + 1) * SLOT], src, AF.Exp,
                     bias=sb["nub"][:, 0:1],
                     reads=[srck, ("nub",)], writes=[("Pp", kcc % 2)])

            def back(kcc):
                MM(P, ps["ps_pv"][:], sb["Vatt"][:, kcc * 128:(kcc + 1) * 128],
                   sb["Pp"][:, (kcc % 2) * SLOT:(kcc % 2 + 1) * SLOT], kcc == 0, kcc == nkc - 1,
                   reads=[("Pp", kcc % 2), ("Vatt", kcc)], writes=[("ps_pv",)])
                if kcc == 0:
                    COPY(P, sb["S"][:], sb["Pp"][:, (kcc % 2) * SLOT:(kcc % 2 + 1) * SLOT],
                         reads=[("Pp", kcc % 2)], writes=[("S",)], eng="pool")
                else:
                    TT(P, sb["S"][:], sb["S"][:], sb["Pp"][:, (kcc % 2) * SLOT:(kcc % 2 + 1) * SLOT], OP.add,
                       reads=[("Pp", kcc % 2), ("S",)], writes=[("S",)], eng="pool")

            for kcc in range(nkc):
                if kcc >= 2:
                    back(kcc - 2)
                front(kcc)
            back(nkc - 2)
            back(nkc - 1)
            MM(P, ps["ps_x"][0:1, :], sb["ones128"][:], sb["S"][:], True, True,
               reads=[("ones128",), ("S",)], writes=[("ps_x",)])
            RECIP(P, sb["rl"][0:1, :], ps["ps_x"][0:1, 0:SLOT], reads=[("ps_x",)], writes=[("rl",)])
            MM(P, ps["ps_x"][:, 0:SLOT], sb["ones1r"][:], sb["rl"][0:1, :], True, True,
               reads=[("ones1r",), ("rl",)], writes=[("ps_x",)])
            ACTV(P, sb["rb"][:], ps["ps_x"][:, 0:SLOT], AF.Copy, reads=[("ps_x",)], writes=[("rb",)])
            TT(P, sb["AO"][:, h * SLOT:(h + 1) * SLOT], ps["ps_pv"][:], sb["rb"][:], OP.mult,
               reads=[("ps_pv",), ("rb",)], writes=[("AO", h)])
        for dc2 in range(NDC):
            par = dc2 % 2
            pw = ps["ps_w0"] if par == 0 else ps["ps_w1"]
            pwk = ("ps_w", par)
            for hc in range(8):
                MM(P, pw[:], bw[:, hc * D + dc2 * 128:hc * D + (dc2 + 1) * 128],
                   sb["AO"][:, hc * SLOT:(hc + 1) * SLOT], hc == 0, hc == 7,
                   reads=[("AO", hc), ("bigw", 2 * hc + dc2 // 8)], writes=[pwk])
            ACTV(P, sb["po"][:, par * SLOT:(par + 1) * SLOT], pw[:], AF.Copy,
                 reads=[pwk], writes=[("po", par)])
            DMA(P, "pool", part2[dc2 * 128:(dc2 + 1) * 128, tsl], sb["po"][:, par * SLOT:(par + 1) * SLOT],
                reads=[("po", par)], writes=[("part", s_, dc2)])
    FIN(P, u[0:1, 63:64])


def f1a_sbuf(add):
    sb = {}
    sb["mdiag"] = add("mdiag", [128, 4 * SLOT], BF16)
    sb["ones128"] = add("ones128", [128, 1], F32)
    sb["ones1r"] = add("ones1r", [1, 128], F32)
    sb["u"] = add("u", [1, 64], F32)
    sb["wstage"] = add("wstage", [128, 2048], F32)
    sb["bigw"] = add("bigw", [128, 34816], BF16)
    sb["xb"] = add("xb", [128, NDC * SLOT], BF16)
    sb["tabs"] = add("tabs", [32, 4 * SLOT], BF16)
    sb["rtmp"] = add("rtmp", [32, 2 * SLOT], F32)
    sb["kvds"] = add("kvds", [128, 4 * SLOT], BF16)
    sb["qds"] = add("qds", [128, 4 * SLOT], BF16)
    sb["kout"] = add("kout", [128, 2 * SLOT], BF16)
    sb["vout"] = add("vout", [128, 2 * SLOT], BF16)
    sb["Katt"] = add("Katt", [128, NKC * 128], BF16)
    sb["Vatt"] = add("Vatt", [128, NKC * 128], BF16)
    sb["Qatt"] = add("Qatt", [128, SLOT], BF16)
    sb["Pp"] = add("Pp", [128, 2 * SLOT], BF16)
    sb["tsc"] = add("tsc", [128, 2 * SLOT], F32)
    sb["S"] = add("S", [128, SLOT], F32)
    sb["AO"] = add("AO", [128, 8 * SLOT], BF16)
    sb["sq"] = add("sq", [128, SLOT], F32)
    sb["rl"] = add("rl", [1, SLOT], F32)
    sb["rb"] = add("rb", [128, SLOT], F32)
    sb["nub"] = add("nub", [128, 1], F32)
    sb["po"] = add("po", [128, 2 * SLOT], BF16)
    sb["k2"] = add("k2", [1, 64], F32)
    return sb


F1A_IO = [
    ("xmT", [D, T], BF16, "ExternalInput"),
    ("kvd_bf", [D, 512], BF16, "ExternalInput"),
    ("qd_bf", [D, 512], BF16, "ExternalInput"),
    ("k_u_sl", [512, 512], F32, "ExternalInput"),
    ("q_u_sl", [512, 512], F32, "ExternalInput"),
    ("v_u_sl", [512, 1024], F32, "ExternalInput"),
    ("rope_k_sl", [D, 512], F32, "ExternalInput"),
    ("rope_q_sl", [512, 512], F32, "ExternalInput"),
    ("o_w_sl", [1024, D], F32, "ExternalInput"),
    ("qcos", [32, T], BF16, "ExternalInput"),
    ("qsin", [32, T], BF16, "ExternalInput"),
    ("kcos", [32, T], BF16, "ExternalInput"),
    ("ksin", [32, T], BF16, "ExternalInput"),
    ("mdiag", [128, 4 * SLOT], BF16, "ExternalInput"),
    ("part2", [D, T], BF16, "ExternalOutput"),
]
F1A_INT = [("k_hbm", [1024, T], BF16), ("v_hbm", [T, 1024], BF16), ("q_hbm", [1024, T], BF16)]


def build_f1a():
    return build_phase(emit_f1a, F1A_IO, f1a_sbuf, F1A_INT)


def f1a_in_maps(inputs, xm, kvd_bf16, qd_bf16):
    import ml_dtypes
    BF = ml_dtypes.bfloat16
    ku = np.asarray(inputs["L1_ffn_k_u_w"], np.float32)
    qu = np.asarray(inputs["L1_ffn_q_u_w"], np.float32)
    vu = np.asarray(inputs["L1_ffn_v_u_w"], np.float32)
    rk = np.asarray(inputs["L1_ffn_rope_k_w"], np.float32)
    rq = np.asarray(inputs["L1_ffn_rope_q_w"], np.float32)
    ow = np.asarray(inputs["L1_ffn_o_w"], np.float32)
    qc, qs = rope_tables(64, 32, 1.0 / math.sqrt(128.0))
    kc, ks = rope_tables(64, 32, 1.0)
    md = host_mdiag()
    maps = []
    for c in range(8):
        b, H = c // 2, c % 2
        maps.append({
            "xmT": np.ascontiguousarray(np.asarray(xm[b]).T.astype(BF)),
            "kvd_bf": kvd_bf16, "qd_bf": qd_bf16,
            "k_u_sl": np.ascontiguousarray(ku[:, H * 512:(H + 1) * 512]),
            "q_u_sl": np.ascontiguousarray(qu[:, H * 512:(H + 1) * 512]),
            "v_u_sl": np.ascontiguousarray(vu[:, H * 1024:(H + 1) * 1024]),
            "rope_k_sl": np.ascontiguousarray(rk[:, H * 512:(H + 1) * 512]),
            "rope_q_sl": np.ascontiguousarray(rq[:, H * 512:(H + 1) * 512]),
            "o_w_sl": np.ascontiguousarray(ow[H * 1024:(H + 1) * 1024, :]),
            "qcos": qc, "qsin": qs, "kcos": kc, "ksin": ks, "mdiag": md,
        })
    return maps



def emit_f1b(P, nc, io, sb, ps):
    MEMSET(P, sb["ones128"][:], 1.0, writes=[("ones128",)])
    MEMSET(P, sb["ones1r"][:], 1.0, writes=[("ones1r",)])
    DMA(P, "syn", sb["ln2"][:], io["ln2_fm"], writes=[("ln2",)])
    DMA(P, "syn", sb["id128"][:], io["id128"], writes=[("id128",)])
    DMA(P, "syn", sb["rbrow"][0:1, :], io["router_b"], writes=[("rbrow",)])
    wst = sb["wstage"]
    for dc in range(NDC):
        p = dc % 2
        st = wst[:, p * 1024:p * 1024 + 7]
        DMA(P, "syn", st, io["router_w"][dc * 128:(dc + 1) * 128, :], writes=stg_keys(p, 7))
        COPY(P, sb["rw"][:, dc * 7:(dc + 1) * 7], st, reads=stg_keys(p, 7), writes=[("rw", dc)])
    MM(P, ps["ps_x"][:, 0:7], sb["ones1r"][:], sb["rbrow"][0:1, :], True, True,
       reads=[("ones1r",), ("rbrow",)], writes=[("ps_x",)])
    ACTV(P, sb["rbb"][:], ps["ps_x"][:, 0:7], AF.Copy, reads=[("ps_x",)], writes=[("rbb",)])

    h_hbm = io["h_hbm"]
    u = sb["u"]
    fused = "par_sum" in io
    if fused:
        DMA(P, "syn", sb["ind2"][0:1, :], io["ind2"], writes=[("ind2",)])
        MM(P, ps["ps_x"][:, 0:2], sb["ones1r"][:], sb["ind2"][0:1, :], True, True,
           reads=[("ones1r",), ("ind2",)], writes=[("ps_x",)])
        ACTV(P, sb["i2b"][:], ps["ps_x"][:, 0:2], AF.Copy, reads=[("ps_x",)], writes=[("i2b",)])
    for os_ in range(4):
        tsl = slice(os_ * SLOT, (os_ + 1) * SLOT)
        for dc in range(NDC):
            j = dc % 2
            pa_t = sb["pstg"][:, j * 2 * SLOT:j * 2 * SLOT + SLOT]
            pb_t = sb["pstg"][:, j * 2 * SLOT + SLOT:j * 2 * SLOT + 2 * SLOT]
            xst = sb["xstg"][:, j * SLOT:(j + 1) * SLOT]
            if fused:
                par_ = io["par_sum"]
                DMA(P, "syn", pa_t, par_[dc * 128:(dc + 1) * 128, os_ * SLOT:(os_ + 1) * SLOT],
                    writes=[("pstg", j, 0)])
                DMA(P, "syn", pb_t, par_[dc * 128:(dc + 1) * 128, TC + os_ * SLOT:TC + (os_ + 1) * SLOT],
                    writes=[("pstg", j, 1)])
            else:
                DMA(P, "syn", pa_t, io["pa"][dc * 128:(dc + 1) * 128, tsl], writes=[("pstg", j, 0)])
                DMA(P, "syn", pb_t, io["pb"][dc * 128:(dc + 1) * 128, tsl], writes=[("pstg", j, 1)])
            DMA(P, "syn", xst, io["x3T"][dc * 128:(dc + 1) * 128, tsl], writes=[("xstg", j)])
            ht = sb["htile"][:, j * SLOT:(j + 1) * SLOT]
            if fused:
                TS(P, ht, pa_t, sb["i2b"][:, 0:1], None, OP.mult,
                   reads=[("pstg", j, 0), ("i2b",)], writes=[("htile", j)])
                STT(P, ht, pb_t, sb["i2b"][:, 1:2], ht, OP.mult, OP.add,
                    reads=[("pstg", j, 1), ("i2b",), ("htile", j)], writes=[("htile", j)])
            else:
                TT(P, ht, pa_t, pb_t, OP.add, reads=[("pstg", j, 0), ("pstg", j, 1)], writes=[("htile", j)])
            TT(P, ht, ht, xst, OP.add, reads=[("xstg", j), ("htile", j)], writes=[("htile", j)])
            ACTV(P, sb["sq"][:], ht, AF.Square, reads=[("htile", j)], writes=[("sq",)])
            MM(P, ps["ps_x"][0:1, :], sb["ones128"][:], sb["sq"][:], dc == 0, dc == NDC - 1,
               reads=[("ones128",), ("sq",)], writes=[("ps_x",)], incall=True)
            DMA(P, "pool", h_hbm[dc * 128:(dc + 1) * 128, tsl], ht,
                reads=[("htile", j)], writes=[("hhbm", os_, dc)])
        TS(P, sb["lnr"][0:1, 0:SLOT], ps["ps_x"][0:1, 0:SLOT], 1.0 / 2048.0, None, OP.mult,
           reads=[("ps_x",)], writes=[("lnr", 0)])
        MEMSET(P, sb["epst"][:], 1e-5, writes=[("epst",)])
        ACTV(P, sb["lnr"][0:1, SLOT:2 * SLOT], sb["lnr"][0:1, 0:SLOT], AF.Sqrt, bias=sb["epst"][0:1, 0:1],
             reads=[("lnr", 0), ("epst",)], writes=[("lnr", 1)])
        RECIP(P, sb["lnr"][0:1, 2 * SLOT:3 * SLOT], sb["lnr"][0:1, SLOT:2 * SLOT],
              reads=[("lnr", 1)], writes=[("lnr", 2)])
        MM(P, ps["ps_x"][:, 0:SLOT], sb["ones1r"][:], sb["lnr"][0:1, 2 * SLOT:3 * SLOT], True, True,
           reads=[("ones1r",), ("lnr", 2)], writes=[("ps_x",)])
        ACTV(P, sb["lna"][:], ps["ps_x"][:, 0:SLOT], AF.Copy, reads=[("ps_x",)], writes=[("lna",)])
        for dc in range(NDC):
            j = dc % 2
            xst = sb["xstg"][:, j * SLOT:(j + 1) * SLOT]
            DMA(P, "syn", xst, h_hbm[dc * 128:(dc + 1) * 128, tsl],
                reads=[("hhbm", os_, dc)], writes=[("xstg", j)])
            xm = sb["xm2"][:, dc * SLOT:(dc + 1) * SLOT]
            TT(P, sb["tsc"][:], xst, sb["lna"][:], OP.mult, reads=[("xstg", j), ("lna",)], writes=[("tsc", 0)])
            TS(P, xm, sb["tsc"][:], sb["ln2"][:, dc:dc + 1], None, OP.mult,
               reads=[("ln2",), ("tsc", 0)], writes=[("xm2", dc)])
        for tokc in range(4):
            for dc in range(NDC):
                MM(P, ps["ps_x"][0:128, 0:7], sb["xm2"][:, dc * SLOT + tokc * 128:dc * SLOT + (tokc + 1) * 128],
                   sb["rw"][:, dc * 7:(dc + 1) * 7], dc == 0, dc == NDC - 1,
                   reads=[("xm2", dc), ("rw", dc)], writes=[("ps_x",)])
            TT(P, sb["probs"][:], ps["ps_x"][0:128, 0:7], sb["rbb"][:], OP.add,
               reads=[("ps_x",), ("rbb",)], writes=[("probs",)])
            ACTV(P, sb["probs"][:], sb["probs"][:], AF.Sigmoid, reads=[("probs",)], writes=[("probs",)])
            REDMAX(P, sb["m1"][:, 0:1], sb["probs"][:], reads=[("probs",)], writes=[("m1",)])
            TS(P, sb["eq1"][:], sb["probs"][:], sb["m1"][:, 0:1], None, OP.is_ge,
               reads=[("probs",), ("m1",)], writes=[("eq1",)])
            STT(P, sb["pw2"][:], sb["eq1"][:], -2.0, sb["probs"][:], OP.mult, OP.add,
                reads=[("eq1",), ("probs",)], writes=[("pw2",)])
            REDMAX(P, sb["m1"][:, 1:2], sb["pw2"][:], reads=[("pw2",)], writes=[("m2",)])
            TS(P, sb["eq2"][:], sb["pw2"][:], sb["m1"][:, 1:2], None, OP.is_ge,
               reads=[("pw2",), ("m2",)], writes=[("eq2",)])
            TT(P, sb["m1"][:, 2:3], sb["m1"][:, 0:1], sb["m1"][:, 1:2], OP.add,
               reads=[("m1",), ("m2",)], writes=[("m12",)])
            RECIP(P, sb["m1"][:, 3:4], sb["m1"][:, 2:3], reads=[("m12",)], writes=[("r12",)])
            TT(P, sb["eq1"][:], sb["eq1"][:], sb["eq2"][:], OP.add, reads=[("eq2",), ("eq1",)], writes=[("eq1",)])
            TT(P, sb["eq1"][:], sb["eq1"][:], sb["probs"][:], OP.mult, reads=[("probs",), ("eq1",)], writes=[("eq1",)])
            TS(P, sb["wtok"][:], sb["eq1"][:], sb["m1"][:, 3:4], None, OP.mult,
               reads=[("eq1",), ("r12",)], writes=[("wtok",)])
            for e in range(7):
                MM(P, ps["ps_x"][0:1, 0:128], sb["wtok"][:, e:e + 1], sb["id128"][:], True, True,
                   reads=[("wtok",), ("id128",)], writes=[("ps_x",)])
                COPY(P, sb["wre"][0:1, e * SLOT + tokc * 128:e * SLOT + (tokc + 1) * 128],
                     ps["ps_x"][0:1, 0:128], reads=[("ps_x",)], writes=[("wre", e, tokc)])
        for e in range(7):
            MM(P, ps["ps_x"][:, 0:SLOT], sb["ones1r"][:], sb["wre"][0:1, e * SLOT:(e + 1) * SLOT], True, True,
               reads=[("ones1r",)] + [("wre", e, tc) for tc in range(4)], writes=[("ps_x",)])
            ACTV(P, sb["w8"][:, e * SLOT:(e + 1) * SLOT], ps["ps_x"][:, 0:SLOT], AF.Copy, scale=0.125,
                 reads=[("ps_x",)], writes=[("w8", e)])

        for grp in range(2):
            for eg in range(4):
                e = grp * 4 + eg
                for dmc in range(NDC):
                    src = io["sh_gate_bf"] if e == 0 else io["r_gate_bf"]
                    ro = dmc * 128 if e == 0 else (e - 1) * D + dmc * 128
                    DMA(P, "syn", sb["wres"][:, dmc * 1024:(dmc + 1) * 1024], src[ro:ro + 128, :],
                        writes=[("wres", dmc)])
                for ic in range(8):
                    par = ic % 2
                    pg = ps["ps_w0"] if par == 0 else ps["ps_w1"]
                    pgk = ("ps_w", par)
                    for dmc in range(NDC):
                        MM(P, pg[:], sb["wres"][:, dmc * 1024 + ic * 128:dmc * 1024 + (ic + 1) * 128],
                           sb["xm2"][:, dmc * SLOT:(dmc + 1) * SLOT], dmc == 0, dmc == NDC - 1,
                           reads=[("wres", dmc), ("xm2", dmc)], writes=[pgk])
                    ACTV(P, sb["sg"][:, ic * SLOT:(ic + 1) * SLOT], pg[:], AF.Silu,
                         reads=[pgk], writes=[("sg", ic)])
                for dmc in range(NDC):
                    src = io["sh_up_bf"] if e == 0 else io["r_up_bf"]
                    ro = dmc * 128 if e == 0 else (e - 1) * D + dmc * 128
                    DMA(P, "syn", sb["wres"][:, dmc * 1024:(dmc + 1) * 1024], src[ro:ro + 128, :],
                        writes=[("wres", dmc)])
                for ic in range(8):
                    par = ic % 2
                    pg = ps["ps_w0"] if par == 0 else ps["ps_w1"]
                    pgk = ("ps_w", par)
                    for dmc in range(NDC):
                        MM(P, pg[:], sb["wres"][:, dmc * 1024 + ic * 128:dmc * 1024 + (ic + 1) * 128],
                           sb["xm2"][:, dmc * SLOT:(dmc + 1) * SLOT], dmc == 0, dmc == NDC - 1,
                           reads=[("wres", dmc), ("xm2", dmc)], writes=[pgk])
                    hhd = sb["hh"][:, (eg * 8 + ic) * SLOT:(eg * 8 + ic + 1) * SLOT]
                    TT(P, sb["tsc"][:], sb["sg"][:, ic * SLOT:(ic + 1) * SLOT], pg[:], OP.mult,
                       reads=[("sg", ic), pgk], writes=[("tsc", 0)])
                    if e == 0:
                        TS(P, hhd, sb["tsc"][:], 0.125, None, OP.mult,
                           reads=[("tsc", 0)], writes=[("hh", eg, ic)])
                    else:
                        TT(P, hhd, sb["tsc"][:], sb["w8"][:, (e - 1) * SLOT:e * SLOT], OP.mult,
                           reads=[("tsc", 0), ("w8", e - 1)], writes=[("hh", eg, ic)])
            PSD = [ps["ps_qk0"], ps["ps_qk1"], ps["ps_m0"], ps["ps_m1"]]
            PSDK = [("ps_qk", 0), ("ps_qk", 1), ("ps_m", 0), ("ps_m", 1)]
            for dc2g in range(4):
                nmm = 0
                for eg in range(4):
                    e = grp * 4 + eg
                    for ic in range(8):
                        par = (eg * 8 + ic) % 2
                        dst = sb["dstg"][:, par * SLOT:(par + 1) * SLOT]
                        src = io["sh_down_bf"] if e == 0 else io["r_down_bf"]
                        ro = ic * 128 if e == 0 else (e - 1) * 1024 + ic * 128
                        DMA(P, "syn", dst, src[ro:ro + 128, dc2g * SLOT:(dc2g + 1) * SLOT],
                            writes=[("dstg", par)])
                        for q_ in range(4):
                            dc2 = dc2g * 4 + q_
                            MM(P, PSD[q_][:], dst[:, q_ * 128:(q_ + 1) * 128],
                               sb["hh"][:, (eg * 8 + ic) * SLOT:(eg * 8 + ic + 1) * SLOT],
                               nmm == 0, nmm == 31,
                               reads=[("dstg", par), ("hh", eg, ic)], writes=[PSDK[q_]],
                               incall=(q_ == 3))
                        nmm += 1
                for q_ in range(4):
                    dc2 = dc2g * 4 + q_
                    mt = sb["macc"][:, dc2 * SLOT:(dc2 + 1) * SLOT]
                    if grp == 0:
                        COPY(P, mt, PSD[q_][:], reads=[PSDK[q_]], writes=[("macc", dc2)])
                    else:
                        TT(P, mt, mt, PSD[q_][:], OP.add, reads=[PSDK[q_], ("macc", dc2)], writes=[("macc", dc2)])
        for dc in range(NDC):
            j = dc % 2
            xst = sb["xstg"][:, j * SLOT:(j + 1) * SLOT]
            DMA(P, "syn", xst, h_hbm[dc * 128:(dc + 1) * 128, tsl],
                reads=[("hhbm", os_, dc)], writes=[("xstg", j)])
            mt = sb["macc"][:, dc * SLOT:(dc + 1) * SLOT]
            TT(P, mt, mt, xst, OP.add, reads=[("xstg", j), ("macc", dc)], writes=[("macc", dc)])
            DMA(P, "syn", xst, io["x3T"][dc * 128:(dc + 1) * 128, tsl], writes=[("xstg", j)])
            TT(P, mt, mt, xst, OP.add, reads=[("xstg", j), ("macc", dc)], writes=[("macc", dc)])
        ln_stats(P, sb, ps, lambda i: sb["macc"][:, i * SLOT:(i + 1) * SLOT],
                 lambda i: ("macc", i), NDC, 1e-5, "f1b")
        for dc in range(NDC):
            mt = sb["macc"][:, dc * SLOT:(dc + 1) * SLOT]
            xo = sb["xot"][:, (dc % 2) * SLOT:(dc % 2 + 1) * SLOT]
            TT(P, xo, mt, sb["lna"][:], OP.mult, reads=[("lna",), ("macc", dc)], writes=[("xot", dc % 2)])
            TT(P, xo, xo, sb["lnb"][:], OP.add, reads=[("lnb",), ("xot", dc % 2)], writes=[("xot", dc % 2)])
            DMA(P, "pool", io["outT"][dc * 128:(dc + 1) * 128, tsl], xo,
                reads=[("xot", dc % 2)], writes=[("outd", os_, dc)])
    FIN(P, u[0:1, 63:64])


def f1b_sbuf(add):
    sb = {}
    sb["ones128"] = add("ones128", [128, 1], F32)
    sb["ones1r"] = add("ones1r", [1, 128], F32)
    sb["u"] = add("u", [1, 64], F32)
    sb["ln2"] = add("ln2", [128, NDC], F32)
    sb["id128"] = add("id128", [128, 128], F32)
    sb["rbrow"] = add("rbrow", [1, 7], F32)
    sb["rw"] = add("rw", [128, NDC * 7], BF16)
    sb["rbb"] = add("rbb", [128, 7], F32)
    sb["wstage"] = add("wstage", [128, 2048], F32)
    sb["pstg"] = add("pstg", [128, 4 * SLOT], BF16)
    sb["xstg"] = add("xstg", [128, 2 * SLOT], F32)
    sb["htile"] = add("htile", [128, 2 * SLOT], F32)
    sb["xm2"] = add("xm2", [128, NDC * SLOT], BF16)
    sb["probs"] = add("probs", [128, 7], F32)
    sb["m1"] = add("m1", [128, 4], F32)
    sb["eq1"] = add("eq1", [128, 7], F32)
    sb["eq2"] = add("eq2", [128, 7], F32)
    sb["pw2"] = add("pw2", [128, 7], F32)
    sb["wtok"] = add("wtok", [128, 7], F32)
    sb["wre"] = add("wre", [1, 7 * SLOT], F32)
    sb["w8"] = add("w8", [128, 7 * SLOT], BF16)
    sb["wres"] = add("wres", [128, NDC * 1024], BF16)
    sb["sg"] = add("sg", [128, 8 * SLOT], BF16)
    sb["hh"] = add("hh", [128, 32 * SLOT], BF16)
    sb["tsc"] = add("tsc", [128, SLOT], F32)
    sb["dstg"] = add("dstg", [128, 2 * SLOT], BF16)
    sb["macc"] = add("macc", [128, NDC * SLOT], F32)
    sb["sq"] = add("sq", [128, SLOT], F32)
    sb["lnr"] = add("lnr", [1, 8 * SLOT], F32)
    sb["lna"] = add("lna", [128, SLOT], F32)
    sb["lnb"] = add("lnb", [128, SLOT], F32)
    sb["epst"] = add("epst", [128, 1], F32)
    sb["xot"] = add("xot", [128, 2 * SLOT], F32)
    sb["ind2"] = add("ind2", [1, 2], F32)
    sb["i2b"] = add("i2b", [128, 2], F32)
    return sb


F1B_IO = [
    ("pa", [D, TC], BF16, "ExternalInput"),
    ("pb", [D, TC], BF16, "ExternalInput"),
    ("x3T", [D, TC], F32, "ExternalInput"),
    ("ln2_fm", [128, NDC], F32, "ExternalInput"),
    ("router_w", [D, 7], F32, "ExternalInput"),
    ("router_b", [1, 7], F32, "ExternalInput"),
    ("id128", [128, 128], F32, "ExternalInput"),
    ("sh_gate_bf", [D, 1024], BF16, "ExternalInput"),
    ("sh_up_bf", [D, 1024], BF16, "ExternalInput"),
    ("sh_down_bf", [1024, D], BF16, "ExternalInput"),
    ("r_gate_bf", [7 * D, 1024], BF16, "ExternalInput"),
    ("r_up_bf", [7 * D, 1024], BF16, "ExternalInput"),
    ("r_down_bf", [7 * 1024, D], BF16, "ExternalInput"),
    ("outT", [D, TC], F32, "ExternalOutput"),
]
F1B_INT = [("h_hbm", [D, TC], F32)]


def build_f1b():
    return build_phase(emit_f1b, F1B_IO, f1b_sbuf, F1B_INT)



LW_SPECS = [
    ("gate_w", D, D), ("agg_w", D, D), ("bc_w", D, D),
    ("d1_w", D, 8192), ("d2_w", 8192, D),
    ("kv_d_w", D, 512), ("q_d_w", D, 512),
    ("sh_gate", D, 1024), ("sh_up", D, 1024), ("sh_down", 1024, D),
    ("r_gate", 7 * D, 1024), ("r_up", 7 * D, 1024), ("r_down", 7 * 1024, D),
]


def emit_lw(P, nc, io, sb, ps):
    for (name, rows, cols) in LW_SPECS:
        rpc = rows // 8
        nch = rpc // 128
        for ch in range(nch):
            for cb in range(0, cols, 2048):
                w = min(2048, cols - cb)
                p = (ch + cb // 2048) % 2
                st = sb["wstage"][:, p * 2048:p * 2048 + w]
                bo = sb["bfout"][:, p * 2048:p * 2048 + w]
                DMA(P, "syn", st, io[name][ch * 128:(ch + 1) * 128, cb:cb + w],
                    writes=[("st", p)])
                COPY(P, bo, st, reads=[("st", p)], writes=[("bo", p)])
                DMA(P, "pool", io[name + "_bf"][ch * 128:(ch + 1) * 128, cb:cb + w], bo,
                    reads=[("bo", p)], writes=[("out", name, ch, cb)])
    FIN(P, sb["wstage"][0:1, 0:1])


def lw_sbuf(add):
    return {"wstage": add("wstage", [128, 4096], F32), "bfout": add("bfout", [128, 4096], BF16)}


def build_lw():
    iospec = []
    for (name, rows, cols) in LW_SPECS:
        rpc = rows // 8
        iospec.append((name, [rpc, cols], F32, "ExternalInput"))
        iospec.append((name + "_bf", [rpc, cols], BF16, "ExternalOutput"))
    return build_phase(emit_lw, iospec, lw_sbuf, ())


def lw_in_maps(inputs):
    src = {
        "gate_w": inputs["L0_ffn_gate_w"], "agg_w": inputs["L0_ffn_agg_w"], "bc_w": inputs["L0_ffn_bc_w"],
        "d1_w": inputs["L0_ffn_d1_w"], "d2_w": inputs["L0_ffn_d2_w"],
        "kv_d_w": inputs["L1_ffn_kv_d_w"], "q_d_w": inputs["L1_ffn_q_d_w"],
        "sh_gate": inputs["L1_ffn_sh_gate"], "sh_up": inputs["L1_ffn_sh_up"], "sh_down": inputs["L1_ffn_sh_down"],
        "r_gate": np.asarray(inputs["L1_ffn_r_gate"], np.float32).reshape(7 * D, 1024),
        "r_up": np.asarray(inputs["L1_ffn_r_up"], np.float32).reshape(7 * D, 1024),
        "r_down": np.asarray(inputs["L1_ffn_r_down"], np.float32).reshape(7 * 1024, D),
    }
    maps = []
    for c in range(8):
        m = {}
        for (name, rows, cols) in LW_SPECS:
            rpc = rows // 8
            m[name] = np.ascontiguousarray(np.asarray(src[name], np.float32).reshape(rows, cols)[c * rpc:(c + 1) * rpc])
        maps.append(m)
    return maps


def lw_assemble(res):
    out = {}
    for (name, rows, cols) in LW_SPECS:
        out[name] = np.concatenate([res[c][name + "_bf"] for c in range(8)], axis=0)
    return out



def vec2fm(v):
    return np.ascontiguousarray(np.asarray(v, np.float32).reshape(-1, 128).T)


def rep4(fm):
    return np.ascontiguousarray(np.repeat(fm, 4, axis=1))


def _run(nc, maps):
    return run_bass_kernel_spmd(nc, maps, core_ids=list(range(8))).results





PAIRS = [[0, 1], [2, 3], [4, 5], [6, 7]]
ALL8 = [list(range(8))]

W_AG = [
    ("gate_w", D, D), ("agg_w", D, D), ("bc_w", D, D),
    ("kv_d_w", D, 512), ("q_d_w", D, 512),
    ("sh_gate", D, 1024), ("sh_up", D, 1024), ("sh_down", 1024, D),
    ("r_gate", 7 * D, 1024), ("r_up", 7 * D, 1024), ("r_down", 7 * 1024, D),
]


def build_fused():
    nc = bass.Bass("TRN2", target_bir_lowering=False)
    dr = {}

    def din(name, shape, dt=F32):
        dr[name] = nc.dram_tensor(name, list(shape), dt, kind="ExternalInput").ap()
        return dr[name]

    def dint(name, shape, dt, shared=False):
        if shared:
            dr[name] = nc.dram_tensor(name, list(shape), dt, addr_space="Shared").ap()
        else:
            dr[name] = nc.dram_tensor(name, list(shape), dt).ap()
        return dr[name]

    din("x0T", [D, T]); din("x0T_own", [D, TC])
    din("l0_wq_sl", [D, 1024]); din("l0_wk_sl", [D, 256]); din("l0_wv_sl", [D, 256])
    din("l0_wo_sl", [1024, D]); din("l0_wa", [D, 32]); din("l0_wb", [D, 32])
    din("cvec_in", [1, 8]); din("gvec_in", [1, 8]); din("tvec_in", [1, 8])
    din("l0_qcos", [64, T], BF16); din("l0_qsin", [64, T], BF16)
    din("l0_kcos", [64, T], BF16); din("l0_ksin", [64, T], BF16)
    din("jmat", [128, 32]); din("qidx", [1, T]); din("mdiag", [128, 4 * SLOT], BF16)
    din("ind2", [1, 2]); din("ind4", [1, 4])
    for (name, rows, cols) in W_AG:
        din("wsh_" + name, [rows // 8, cols])
    din("d1_sl", [D, 1024]); din("d2_sl", [1024, D])
    for n in ["ing4", "inb4", "png4", "pnb4", "aggb4", "bcb4", "d2b4"]:
        din("f0_" + n, [128, 4 * NDC])
    din("f0_d1b4", [128, 32])
    for n in ["gb", "og", "ob"]:
        din("f0_" + n, [128, NDC])
    din("ln1_fm", [128, NDC]); din("ln2_fm", [128, NDC])
    din("l1_wq_sl", [D, 1024]); din("l1_wk_sl", [D, 256]); din("l1_wv_sl", [D, 256])
    din("l1_wo_sl", [1024, D])
    din("l1_qcos", [32, T], BF16); din("l1_qsin", [32, T], BF16)
    din("l1_kcos", [32, T], BF16); din("l1_ksin", [32, T], BF16)
    din("fa_k_u_sl", [512, 512]); din("fa_q_u_sl", [512, 512]); din("fa_v_u_sl", [512, 1024])
    din("fa_rope_k_sl", [D, 512]); din("fa_rope_q_sl", [512, 512]); din("fa_o_w_sl", [1024, D])
    din("fa_qcos", [32, T], BF16); din("fa_qsin", [32, T], BF16)
    din("fa_kcos", [32, T], BF16); din("fa_ksin", [32, T], BF16)
    din("router_w", [D, 7]); din("router_b", [1, 7]); din("id128", [128, 128])
    outT = nc.dram_tensor("outT", [D, TC], F32, kind="ExternalOutput").ap()

    for (name, rows, cols) in W_AG:
        dint("wag_in_" + name, [rows // 8, cols], BF16)
        dint("wag_out_" + name, [rows, cols], BF16, shared=True)
    dint("d1bf", [D, 1024], BF16); dint("d2bf", [1024, D], BF16)
    dint("qT0", [1024, T], BF16)
    dint("part0", [D, T], BF16); dint("part0s", [D, T], BF16)
    dint("X1T", [D, TC], F32); dint("X2T", [D, TC], F32); dint("X3T", [D, TC], F32)
    dint("xl_in", [128, NDC], F32); dint("xl_out", [1024, NDC], F32, shared=True)
    dint("th_in", [128, 4 * NDC], F32); dint("th_out", [1024, 4 * NDC], F32, shared=True)
    dint("x2cc", [D, TC], BF16); dint("x2ag", [2 * D, TC], BF16)
    dint("qT1", [1024, T], BF16)
    dint("part1", [D, T], BF16); dint("part1s", [D, T], BF16)
    dint("xmcc", [D, TC], BF16); dint("xmag", [2 * D, TC], BF16)
    dint("fa_k_hbm", [1024, T], BF16); dint("fa_v_hbm", [T, 1024], BF16); dint("fa_q_hbm", [1024, T], BF16)
    dint("part2", [D, T], BF16); dint("part2s", [D, T], BF16)
    dint("h_hbm", [D, TC], F32)
    for n in ["dbg_th", "dbg_bc", "dbg_t0"]:
        dint(n, [128, 4 * NDC], F32)

    from contextlib import ExitStack
    with ExitStack() as outer:
        ps = {n: outer.enter_context(nc.psum_tensor(n, [128, 512], F32)) for n in PSUM_NAMES}
        Pr = Prog()
        sems = {n: outer.enter_context(nc.semaphore("sem_" + n)) for n in Pr.sem_names()}
        block = outer.enter_context(nc.Block())

        with ExitStack() as st_:
            add = lambda n, sh, d: st_.enter_context(nc.sbuf_tensor("w_" + n, sh, d))
            sbw = lw_sbuf(add)
            for (name, rows, cols) in W_AG + [("d1_slX", D, 1024), ("d2_slX", 1024, D)]:
                if name.endswith("X"):
                    src = dr[name[:-1]]
                    dst = dr["d1bf" if name.startswith("d1") else "d2bf"]
                    rpc = rows
                else:
                    src = dr["wsh_" + name]
                    dst = dr["wag_in_" + name]
                    rpc = rows // 8
                for ch in range(rpc // 128):
                    for cb in range(0, cols, 2048):
                        w = min(2048, cols - cb)
                        p = (ch + cb // 2048) % 2
                        stg = sbw["wstage"][:, p * 2048:p * 2048 + w]
                        bo = sbw["bfout"][:, p * 2048:p * 2048 + w]
                        DMA(Pr, "syn", stg, src[ch * 128:(ch + 1) * 128, cb:cb + w], writes=[("st", p)])
                        COPY(Pr, bo, stg, reads=[("st", p)], writes=[("bo", p)])
                        DMA(Pr, "pool", dst[ch * 128:(ch + 1) * 128, cb:cb + w], bo,
                            reads=[("bo", p)], writes=[("wout", name, ch, cb)])
            for (name, rows, cols) in W_AG:
                CCOMP(Pr, "AllGather", OP.bypass, [dr["wag_in_" + name].opt()],
                      [dr["wag_out_" + name].opt()], ALL8,
                      reads=[("wout", name, ch, cb) for ch in range(rows // 8 // 128)
                             for cb in range(0, cols, 2048)],
                      writes=[("wag", name)])

        with ExitStack() as st_:
            add = lambda n, sh, d: st_.enter_context(nc.sbuf_tensor("a0_" + n, sh, d))
            sb = l0a_sbuf(add)
            io = {"x0T": dr["x0T"], "wq_sl": dr["l0_wq_sl"], "wk_sl": dr["l0_wk_sl"],
                  "wv_sl": dr["l0_wv_sl"], "wo_sl": dr["l0_wo_sl"], "wa": dr["l0_wa"], "wb": dr["l0_wb"],
                  "cvec_in": dr["cvec_in"], "gvec_in": dr["gvec_in"], "tvec_in": dr["tvec_in"],
                  "qcos": dr["l0_qcos"], "qsin": dr["l0_qsin"], "kcos": dr["l0_kcos"], "ksin": dr["l0_ksin"],
                  "jmat": dr["jmat"], "qidx": dr["qidx"], "mdiag": dr["mdiag"],
                  "part0": dr["part0"], "qT_hbm": dr["qT0"]}
            emit_l0a(Pr, nc, io, sb, ps)
        Pr.phase_fence()
        CCOMP(Pr, "AllReduce", OP.add, [dr["part0"].opt()], [dr["part0s"].opt()], PAIRS)
        Pr.phase_fence()

        with ExitStack() as st_:
            add = lambda n, sh, d: st_.enter_context(nc.sbuf_tensor("f1_" + n, sh, d))
            sb = fin_sbuf(add, False)
            io = {"par_sum": dr["part0s"], "ind2": dr["ind2"], "xprev": dr["x0T_own"],
                  "xout": dr["X1T"], "xl_cc": dr["xl_in"]}
            emit_fin(Pr, nc, io, sb, ps, False)
        Pr.phase_fence()
        CCOMP(Pr, "AllGather", OP.bypass, [dr["xl_in"].opt()], [dr["xl_out"].opt()], ALL8)
        Pr.phase_fence()

        with ExitStack() as st_:
            add = lambda n, sh, d: st_.enter_context(nc.sbuf_tensor("s0_" + n, sh, d))
            sb = f0_sbuf(add)
            io = {"x1T": dr["X1T"], "xlg": dr["xl_out"], "ind": dr["ind4"],
                  "gate_w_bf": dr["wag_out_gate_w"], "agg_bf": dr["wag_out_agg_w"],
                  "bc_bf": dr["wag_out_bc_w"], "d1_bf": dr["d1bf"], "d2_bf": dr["d2bf"],
                  "cc_in": dr["th_in"], "cc_out": dr["th_out"],
                  "x2T": dr["X2T"], "x2cc": dr["x2cc"],
                  "dbg_th": dr["dbg_th"], "dbg_bc": dr["dbg_bc"], "dbg_t0": dr["dbg_t0"]}
            for n in ["ing4", "inb4", "png4", "pnb4", "aggb4", "bcb4", "d2b4", "d1b4", "gb", "og", "ob"]:
                io[n] = dr["f0_" + n]
            emit_f0(Pr, nc, io, sb, ps)
        Pr.phase_fence()
        CCOMP(Pr, "AllGather", OP.bypass, [dr["x2cc"].opt()], [dr["x2ag"].opt()], PAIRS)
        Pr.phase_fence()

        with ExitStack() as st_:
            add = lambda n, sh, d: st_.enter_context(nc.sbuf_tensor("a1_" + n, sh, d))
            sb = l1a_sbuf(add)
            x2ag = dr["x2ag"]
            io = {"xsrc": lambda s_: (x2ag[(s_ // 4) * D:(s_ // 4 + 1) * D, :], (s_ % 4) * SLOT),
                  "x_bf16": True,
                  "wq_sl": dr["l1_wq_sl"], "wk_sl": dr["l1_wk_sl"], "wv_sl": dr["l1_wv_sl"],
                  "wo_sl": dr["l1_wo_sl"],
                  "qcos": dr["l1_qcos"], "qsin": dr["l1_qsin"], "kcos": dr["l1_kcos"], "ksin": dr["l1_ksin"],
                  "mdiag": dr["mdiag"], "part1": dr["part1"], "qT_hbm": dr["qT1"]}
            emit_l1a(Pr, nc, io, sb, ps)
        Pr.phase_fence()
        CCOMP(Pr, "AllReduce", OP.add, [dr["part1"].opt()], [dr["part1s"].opt()], PAIRS)
        Pr.phase_fence()

        with ExitStack() as st_:
            add = lambda n, sh, d: st_.enter_context(nc.sbuf_tensor("f2_" + n, sh, d))
            sb = fin_sbuf(add, True)
            io = {"par_sum": dr["part1s"], "ind2": dr["ind2"], "xprev": dr["X2T"],
                  "xout": dr["X3T"], "ln1_fm": dr["ln1_fm"], "xm": dr["xmcc"]}
            emit_fin(Pr, nc, io, sb, ps, True, 1e-5)
        Pr.phase_fence()
        CCOMP(Pr, "AllGather", OP.bypass, [dr["xmcc"].opt()], [dr["xmag"].opt()], PAIRS)
        Pr.phase_fence()

        with ExitStack() as st_:
            add = lambda n, sh, d: st_.enter_context(nc.sbuf_tensor("sa_" + n, sh, d))
            sb = f1a_sbuf(add)
            xmag = dr["xmag"]
            io = {"xsrc": lambda s_: (xmag[(s_ // 4) * D:(s_ // 4 + 1) * D, :], (s_ % 4) * SLOT),
                  "kvd_bf": dr["wag_out_kv_d_w"], "qd_bf": dr["wag_out_q_d_w"],
                  "k_u_sl": dr["fa_k_u_sl"], "q_u_sl": dr["fa_q_u_sl"], "v_u_sl": dr["fa_v_u_sl"],
                  "rope_k_sl": dr["fa_rope_k_sl"], "rope_q_sl": dr["fa_rope_q_sl"],
                  "o_w_sl": dr["fa_o_w_sl"],
                  "qcos": dr["fa_qcos"], "qsin": dr["fa_qsin"], "kcos": dr["fa_kcos"], "ksin": dr["fa_ksin"],
                  "mdiag": dr["mdiag"], "part2": dr["part2"],
                  "k_hbm": dr["fa_k_hbm"], "v_hbm": dr["fa_v_hbm"], "q_hbm": dr["fa_q_hbm"]}
            emit_f1a(Pr, nc, io, sb, ps)
        Pr.phase_fence()
        CCOMP(Pr, "AllReduce", OP.add, [dr["part2"].opt()], [dr["part2s"].opt()], PAIRS)
        Pr.phase_fence()

        with ExitStack() as st_:
            add = lambda n, sh, d: st_.enter_context(nc.sbuf_tensor("fb_" + n, sh, d))
            sb = f1b_sbuf(add)
            io = {"par_sum": dr["part2s"], "ind2": dr["ind2"], "x3T": dr["X3T"],
                  "ln2_fm": dr["ln2_fm"], "router_w": dr["router_w"], "router_b": dr["router_b"],
                  "id128": dr["id128"],
                  "sh_gate_bf": dr["wag_out_sh_gate"], "sh_up_bf": dr["wag_out_sh_up"],
                  "sh_down_bf": dr["wag_out_sh_down"],
                  "r_gate_bf": dr["wag_out_r_gate"], "r_up_bf": dr["wag_out_r_up"],
                  "r_down_bf": dr["wag_out_r_down"],
                  "h_hbm": dr["h_hbm"], "outT": outT}
            emit_f1b(Pr, nc, io, sb, ps)

        replay(Pr, block, sems)
    return nc


def fused_in_maps(inputs):
    x = np.asarray(inputs["L0_attn_x"], np.float32)
    l0maps = l0a_in_maps(inputs)
    l1Wq = np.asarray(inputs["L1_attn_Wq"], np.float32)
    l1Wk = np.asarray(inputs["L1_attn_Wk"], np.float32)
    l1Wv = np.asarray(inputs["L1_attn_Wv"], np.float32)
    l1Wo = np.asarray(inputs["L1_attn_Wo"], np.float32)
    qc1, qs1 = rope_tables(64, 32, 1.0 / math.sqrt(64.0))
    kc1, ks1 = rope_tables(64, 32, 1.0)
    qcf, qsf = rope_tables(64, 32, 1.0 / math.sqrt(128.0))
    kcf, ksf = rope_tables(64, 32, 1.0)
    ku = np.asarray(inputs["L1_ffn_k_u_w"], np.float32)
    qu = np.asarray(inputs["L1_ffn_q_u_w"], np.float32)
    vu = np.asarray(inputs["L1_ffn_v_u_w"], np.float32)
    rk = np.asarray(inputs["L1_ffn_rope_k_w"], np.float32)
    rq = np.asarray(inputs["L1_ffn_rope_q_w"], np.float32)
    ow = np.asarray(inputs["L1_ffn_o_w"], np.float32)
    g = {k: inputs["L0_ffn_" + k] for k in ["in_gamma", "in_beta", "agg_b", "pn_gamma", "pn_beta",
                                            "d1_b", "d2_b", "bc_b", "gate_b", "out_gamma", "out_beta"]}
    wsrc = {
        "gate_w": inputs["L0_ffn_gate_w"], "agg_w": inputs["L0_ffn_agg_w"], "bc_w": inputs["L0_ffn_bc_w"],
        "kv_d_w": inputs["L1_ffn_kv_d_w"], "q_d_w": inputs["L1_ffn_q_d_w"],
        "sh_gate": inputs["L1_ffn_sh_gate"], "sh_up": inputs["L1_ffn_sh_up"], "sh_down": inputs["L1_ffn_sh_down"],
        "r_gate": np.asarray(inputs["L1_ffn_r_gate"], np.float32).reshape(7 * D, 1024),
        "r_up": np.asarray(inputs["L1_ffn_r_up"], np.float32).reshape(7 * D, 1024),
        "r_down": np.asarray(inputs["L1_ffn_r_down"], np.float32).reshape(7 * 1024, D),
    }
    d1 = np.asarray(inputs["L0_ffn_d1_w"], np.float32)
    d2 = np.asarray(inputs["L0_ffn_d2_w"], np.float32)
    maps = []
    for c in range(8):
        b, H = c // 2, c % 2
        own = slice(H * TC, (H + 1) * TC)
        m0 = l0maps[c]
        ind2 = np.zeros((1, 2), np.float32); ind2[0, H] = 1.0
        ind4 = np.zeros((1, 4), np.float32); ind4[0, b] = 1.0
        m = {
            "x0T": m0["x0T"], "x0T_own": np.ascontiguousarray(x[b].T[:, own]),
            "l0_wq_sl": m0["wq_sl"], "l0_wk_sl": m0["wk_sl"], "l0_wv_sl": m0["wv_sl"],
            "l0_wo_sl": m0["wo_sl"], "l0_wa": m0["wa"], "l0_wb": m0["wb"],
            "cvec_in": m0["cvec_in"], "gvec_in": m0["gvec_in"], "tvec_in": m0["tvec_in"],
            "l0_qcos": m0["qcos"], "l0_qsin": m0["qsin"], "l0_kcos": m0["kcos"], "l0_ksin": m0["ksin"],
            "jmat": m0["jmat"], "qidx": m0["qidx"], "mdiag": m0["mdiag"],
            "ind2": ind2, "ind4": ind4,
            "d1_sl": np.ascontiguousarray(d1[:, c * 1024:(c + 1) * 1024]),
            "d2_sl": np.ascontiguousarray(d2[c * 1024:(c + 1) * 1024, :]),
            "f0_ing4": rep4(vec2fm(g["in_gamma"])), "f0_inb4": rep4(vec2fm(g["in_beta"])),
            "f0_png4": rep4(vec2fm(g["pn_gamma"])), "f0_pnb4": rep4(vec2fm(g["pn_beta"])),
            "f0_aggb4": rep4(vec2fm(g["agg_b"])), "f0_bcb4": rep4(vec2fm(g["bc_b"])),
            "f0_d2b4": rep4(vec2fm(g["d2_b"])),
            "f0_d1b4": rep4(vec2fm(np.asarray(g["d1_b"], np.float32)[c * 1024:(c + 1) * 1024])),
            "f0_gb": vec2fm(g["gate_b"]), "f0_og": vec2fm(g["out_gamma"]), "f0_ob": vec2fm(g["out_beta"]),
            "ln1_fm": vec2fm(inputs["L1_ffn_ln1_w"]), "ln2_fm": vec2fm(inputs["L1_ffn_ln2_w"]),
            "l1_wq_sl": np.ascontiguousarray(l1Wq[:, H * 1024:(H + 1) * 1024]),
            "l1_wk_sl": np.ascontiguousarray(l1Wk[:, H * 256:(H + 1) * 256]),
            "l1_wv_sl": np.ascontiguousarray(l1Wv[:, H * 256:(H + 1) * 256]),
            "l1_wo_sl": np.ascontiguousarray(l1Wo[H * 1024:(H + 1) * 1024, :]),
            "l1_qcos": qc1, "l1_qsin": qs1, "l1_kcos": kc1, "l1_ksin": ks1,
            "fa_k_u_sl": np.ascontiguousarray(ku[:, H * 512:(H + 1) * 512]),
            "fa_q_u_sl": np.ascontiguousarray(qu[:, H * 512:(H + 1) * 512]),
            "fa_v_u_sl": np.ascontiguousarray(vu[:, H * 1024:(H + 1) * 1024]),
            "fa_rope_k_sl": np.ascontiguousarray(rk[:, H * 512:(H + 1) * 512]),
            "fa_rope_q_sl": np.ascontiguousarray(rq[:, H * 512:(H + 1) * 512]),
            "fa_o_w_sl": np.ascontiguousarray(ow[H * 1024:(H + 1) * 1024, :]),
            "fa_qcos": qcf, "fa_qsin": qsf, "fa_kcos": kcf, "fa_ksin": ksf,
            "router_w": np.asarray(inputs["L1_ffn_router_w"], np.float32),
            "router_b": np.asarray(inputs["L1_ffn_router_b"], np.float32).reshape(1, 7),
            "id128": np.eye(128, dtype=np.float32),
        }
        for (name, rows, cols) in W_AG:
            rpc = rows // 8
            m["wsh_" + name] = np.ascontiguousarray(
                np.asarray(wsrc[name], np.float32).reshape(rows, cols)[c * rpc:(c + 1) * rpc])
        maps.append(m)
    return maps


def kernel_fused(**inputs):
    inputs = {k: np.asarray(v) for k, v in inputs.items()}
    maps = fused_in_maps(inputs)
    r = _run(build_fused(), maps)
    out = np.empty((B, T, D), np.float32)
    for c in range(8):
        b, H = c // 2, c % 2
        out[b, H * TC:(H + 1) * TC, :] = r[c]["outT"].T
    return out


def kernel(**inputs):
    return kernel_fused(**inputs)


def kernel_unfused(**inputs):
    inputs = {k: np.asarray(v) for k, v in inputs.items()}

    wbf = lw_assemble(_run(build_lw(), lw_in_maps(inputs)))

    r = _run(build_l0a(), l0a_in_maps(inputs))
    part0 = [r[c]["part0"] for c in range(8)]

    x = np.asarray(inputs["L0_attn_x"], np.float32)
    maps = []
    for c in range(8):
        b, H = c // 2, c % 2
        own = slice(H * TC, (H + 1) * TC)
        maps.append({
            "pa": np.ascontiguousarray(part0[2 * b + H][:, own]),
            "pb": np.ascontiguousarray(part0[2 * b + 1 - H][:, own]),
            "xprev": np.ascontiguousarray(x[b].T[:, own]),
        })
    r = _run(build_fin(False), maps)
    X1T = [r[c]["xout"] for c in range(8)]

    xlast4 = np.stack([X1T[2 * b + 1][:, TC - 1].reshape(NDC, 128).T for b in range(4)], axis=2).reshape(128, 64)
    g = {k: inputs["L0_ffn_" + k] for k in ["in_gamma", "in_beta", "agg_b", "pn_gamma", "pn_beta",
                                            "d1_b", "d2_b", "bc_b", "gate_b", "out_gamma", "out_beta"]}
    maps = []
    for c in range(8):
        b, H = c // 2, c % 2
        ind = np.zeros((1, 4), np.float32); ind[0, b] = 1.0
        maps.append({
            "x1T": X1T[c],
            "xlast4": np.ascontiguousarray(xlast4),
            "ing4": rep4(vec2fm(g["in_gamma"])), "inb4": rep4(vec2fm(g["in_beta"])),
            "png4": rep4(vec2fm(g["pn_gamma"])), "pnb4": rep4(vec2fm(g["pn_beta"])),
            "aggb4": rep4(vec2fm(g["agg_b"])), "bcb4": rep4(vec2fm(g["bc_b"])),
            "gb": vec2fm(g["gate_b"]), "og": vec2fm(g["out_gamma"]), "ob": vec2fm(g["out_beta"]),
            "ind": ind,
            "d1b4": rep4(vec2fm(np.asarray(g["d1_b"], np.float32)[c * 1024:(c + 1) * 1024])),
            "d2b4": rep4(vec2fm(g["d2_b"])),
            "gate_w_bf": wbf["gate_w"],
            "agg_bf": wbf["agg_w"],
            "bc_bf": wbf["bc_w"],
            "d1_bf": np.ascontiguousarray(wbf["d1_w"][:, c * 1024:(c + 1) * 1024]),
            "d2_bf": np.ascontiguousarray(wbf["d2_w"][c * 1024:(c + 1) * 1024, :]),
        })
    r = _run(build_f0(), maps)
    X2T = [r[c]["x2T"] for c in range(8)]

    X2 = [np.concatenate([X2T[2 * b], X2T[2 * b + 1]], axis=1).T for b in range(4)]
    r = _run(build_l1a(), l1a_in_maps(inputs, np.stack(X2)))
    part1 = [r[c]["part1"] for c in range(8)]

    maps = []
    for c in range(8):
        b, H = c // 2, c % 2
        own = slice(H * TC, (H + 1) * TC)
        maps.append({
            "pa": np.ascontiguousarray(part1[2 * b + H][:, own]),
            "pb": np.ascontiguousarray(part1[2 * b + 1 - H][:, own]),
            "xprev": np.ascontiguousarray(X2T[c]),
            "ln1_fm": vec2fm(inputs["L1_ffn_ln1_w"]),
        })
    r = _run(build_fin(True, 1e-5), maps)
    X3T = [r[c]["xout"] for c in range(8)]
    xmT = [r[c]["xm"] for c in range(8)]

    import ml_dtypes
    xm_full = [np.concatenate([xmT[2 * b], xmT[2 * b + 1]], axis=1).T for b in range(4)]
    maps = f1a_in_maps(inputs, np.stack(xm_full), wbf["kv_d_w"], wbf["q_d_w"])
    r = _run(build_f1a(), maps)
    part2 = [r[c]["part2"] for c in range(8)]

    rg, ru, rd = wbf["r_gate"], wbf["r_up"], wbf["r_down"]
    maps = []
    for c in range(8):
        b, H = c // 2, c % 2
        own = slice(H * TC, (H + 1) * TC)
        maps.append({
            "pa": np.ascontiguousarray(part2[2 * b + H][:, own]),
            "pb": np.ascontiguousarray(part2[2 * b + 1 - H][:, own]),
            "x3T": X3T[c],
            "ln2_fm": vec2fm(inputs["L1_ffn_ln2_w"]),
            "router_w": np.asarray(inputs["L1_ffn_router_w"], np.float32),
            "router_b": np.asarray(inputs["L1_ffn_router_b"], np.float32).reshape(1, 7),
            "id128": np.eye(128, dtype=np.float32),
            "sh_gate_bf": wbf["sh_gate"], "sh_up_bf": wbf["sh_up"], "sh_down_bf": wbf["sh_down"],
            "r_gate_bf": rg, "r_up_bf": ru, "r_down_bf": rd,
        })
    r = _run(build_f1b(), maps)
    out = np.empty((B, T, D), np.float32)
    for c in range(8):
        b, H = c // 2, c % 2
        out[b, H * TC:(H + 1) * TC, :] = r[c]["outT"].T
    return out

